# version 04 (final)
# baseline (speedup 1.0000x reference)
import os
os.environ.setdefault("NEURON_SCRATCHPAD_PAGE_SIZE", "512")
import numpy as np
import concourse.bass as bass
import concourse.mybir as mybir
from concourse.bass_utils import run_bass_kernel_spmd
from concourse.bass_core_rust import IndirectOffsetOnAxis
from concourse.bass_primitives_rust import RegisterHandles
from concourse.expressions import RuntimeValue
import ml_dtypes

F32 = mybir.dt.float32
BF16 = mybir.dt.bfloat16
I32 = mybir.dt.int32
AF = mybir.ActivationFunctionType
ALU = mybir.AluOpType
AX = mybir.AxisListType

T = 8192
D = 1024
NH0 = 4
DK = 256
NT1 = T // 128
NT5 = T // 512
E = 8
NDBL = 6
EG_NSTEP = 9
H_NSTEP = 4
H_NE2 = 8
H_ACC = True
H_P2MM = True
H_P2EV = True
H_P2LD = 7
DH = 4096
ENGS = ("sync", "tensor", "vector", "scalar", "gpsimd")

INTER = {
    "xn_fm":  ([8, 128, T], BF16),
    "q_fm":   ([8, 128, T], BF16),
    "k_fm":   ([8, 128, T], BF16),
    "kb_fm":  ([8, 128, T], BF16),
    "v_fm":   ([8, 128, T], BF16),
    "k_tm":   ([NH0, NT1, 128, DK], BF16),
    "kb_tm":  ([NH0, NT1, 128, DK], BF16),
    "vb_tm":  ([NH0, NT1, 128, DK], BF16),
    "beta":   ([NH0, T], F32),
    "delta_fm": ([8, 128, T], BF16),
}


class Ctx:
    def __init__(self, nc, aps):
        self.nc = nc
        self.aps = aps
        self.ss = {}
        self.ssn = {e: 0 for e in ENGS}

    def mark(self, eng, instr):
        self.ssn[eng] += 1
        instr.then_inc(self.ss[eng], 1)
        return self.ssn[eng]

    def wt(self, eng, e, n):
        e.wait_ge(self.ss[eng], n)

    def ridx(self, e, base_sv, const, maxv, slot=0):
        key = (id(e), "ri", slot)
        if not hasattr(self, "_tr"):
            self._tr = {}
        if key not in self._tr:
            self._tr[key] = e.alloc_register("ri_%d_%d" % (len(self._tr), slot))
        r = self._tr[key]
        e.reg_alu(r, base_sv, const, ALU.add)
        return RuntimeValue(val=RegisterHandles(r), min_val=0,
                            max_val=maxv)

    def rwait(self, e, sem_h, base_sv, const):
        key = (id(e), "rw")
        if not hasattr(self, "_tr"):
            self._tr = {}
        if key not in self._tr:
            self._tr[key] = e.alloc_register("rw_%d" % len(self._tr))
        r = self._tr[key]
        e.reg_alu(r, base_sv, const, ALU.add)
        e.wait_ge(sem_h, r)


def phase_consts(ctx, sb, ps, sem):
    aps = ctx.aps
    s_c = sem["c_done"]

    def eng_sync(e):
        e.dma_start(out=sb["eyeb"][:], in_=aps["eyeb"]).then_inc(s_c, 16)

    def wait(e):
        e.wait_ge(s_c, 16)

    return dict(sync=eng_sync, tensor=wait, vector=wait, scalar=wait,
                gpsimd=wait)


def phase_a(ctx, sb, ps, sem):
    aps = ctx.aps
    x = aps["x"]
    xn_fm = aps["xn_fm"]
    xt = sb["a_xt"]
    m1 = sb["a_m1"]
    xc2 = sb["a_xc2"]
    xnb = sb["a_xnb"]
    fmv = sb["a_fmv"]
    eyeb = sb["eyeb"]
    ptp = ps["a_ptp"]
    s_x, s_ln, s_sq, s_tp, s_ev, s_out = (sem[k] for k in
        ("a_x", "a_ln", "a_sq", "a_tp", "a_ev", "a_out"))

    def eng_sync(e):
        for i in range(NT1):
            p = i % 2
            if i >= 2:
                e.wait_ge(s_ln, 3 * (i - 2) + 3)
            e.dma_start(out=xt[:, p], in_=x[i * 128:(i + 1) * 128, :]).then_inc(s_x, 16)

    def eng_vector(e):
        mk = lambda ins: ctx.mark("vector", ins)
        wt = lambda n: ctx.wt("vector", e, n)
        for i in range(NT1):
            p = i % 2
            e.wait_ge(s_x, 16 * (i + 1))
            c1 = mk(e.reduce_sum(m1[:, p, 0:1], xt[:, p], axis=AX.X))
            wt(c1)
            c2 = mk(e.tensor_scalar_mul(m1[:, p, 0:1], m1[:, p, 0:1], 1.0 / D))
            wt(c2)
            e.tensor_scalar(xt[:, p], xt[:, p], m1[:, p, 0:1], None,
                            ALU.subtract).then_inc(s_ln, 1)
            e.wait_ge(s_sq, 2 * i + 1)
            c4 = mk(e.reduce_sum(m1[:, p, 1:2], xc2[:, p], axis=AX.X))
            wt(c4)
            e.tensor_scalar(m1[:, p, 1:2], m1[:, p, 1:2], 1.0 / D, 1e-5,
                            ALU.mult, ALU.add).then_inc(s_ln, 1)
            e.wait_ge(s_sq, 2 * i + 2)
            if i >= 2:
                e.wait_ge(s_tp, i - 1)
            e.tensor_scalar(xnb[:, p], xt[:, p], m1[:, p, 2:3], None,
                            ALU.mult).then_inc(s_ln, 1)
            e.wait_ge(s_tp, i + 1)
            if i >= 2:
                e.wait_ge(s_out, 16 * (i - 1))
            e.tensor_copy(fmv[:, p], ptp[:, p]).then_inc(s_ev, 1)

    def eng_scalar(e):
        mk = lambda ins: ctx.mark("scalar", ins)
        wt = lambda n: ctx.wt("scalar", e, n)
        for i in range(NT1):
            p = i % 2
            e.wait_ge(s_ln, 3 * i + 1)
            e.square(xc2[:, p], xt[:, p]).then_inc(s_sq, 1)
            e.wait_ge(s_ln, 3 * i + 2)
            c1 = mk(e.activation(m1[:, p, 3:4], m1[:, p, 1:2], AF.Ln))
            wt(c1)
            e.activation(m1[:, p, 2:3], m1[:, p, 3:4], AF.Exp,
                         scale=-0.5).then_inc(s_sq, 1)

    def eng_tensor(e):
        for i in range(NT1):
            p = i % 2
            e.wait_ge(s_ln, 3 * i + 3)
            if i >= 2:
                e.wait_ge(s_ev, i - 1)
            for d in range(8):
                mm = e.transpose(ptp[:, p, d], xnb[:, p, d * 128:(d + 1) * 128],
                                 eyeb[:])
            mm.then_inc(s_tp, 1)

    def eng_gpsimd(e):
        dbg = aps.get("dbg_xnb")
        for i in range(NT1):
            p = i % 2
            e.wait_ge(s_ev, i + 1)
            e.dma_start(
                out=xn_fm[:, :, i * 128:(i + 1) * 128].rearrange("d p q -> p d q"),
                in_=fmv[:, p],
            ).then_inc(s_out, 16)
            if dbg is not None:
                e.dma_start(out=dbg[i * 128:(i + 1) * 128, :],
                            in_=xnb[:, p]).then_inc(s_out, 16)
        e.wait_ge(s_out, 16 * NT1 * (2 if dbg is not None else 1))

    return dict(sync=eng_sync, vector=eng_vector, scalar=eng_scalar,
                tensor=eng_tensor, gpsimd=eng_gpsimd)


def make_barrier(tag):
    def phase(ctx, sb, ps, sem):
        s = sem[tag]
        out = {}
        for name in ENGS:
            def fn(e, s=s):
                e.engine_nop().then_inc(s, 1)
                e.wait_ge(s, 5)
            out[name] = fn
        return out
    return phase


def build_program(phase_fns, inputs_spec, outputs_spec, sbuf_spec, psum_spec,
                  sem_names, scratch_spec=None):
    nc = bass.Bass("TRN2", target_bir_lowering=False, debug=False)
    aps = {}
    for name, (shape, dt) in inputs_spec.items():
        aps[name] = nc.dram_tensor(name, shape, dt, kind="ExternalInput").ap()
    for name, (shape, dt) in outputs_spec.items():
        aps[name] = nc.dram_tensor(name, shape, dt, kind="ExternalOutput").ap()
    for name, (shape, dt) in (scratch_spec or {}).items():
        aps[name] = nc.dram_tensor(name, shape, dt, kind="Internal").ap()
    ctx = Ctx(nc, aps)

    import contextlib
    with contextlib.ExitStack() as st:
        sb = {}
        for name, (shape, dt) in sbuf_spec.items():
            sb[name] = st.enter_context(nc.sbuf_tensor("sb_" + name, shape, dt))
        ps = {}
        for name, (shape, dt) in psum_spec.items():
            ps[name] = st.enter_context(nc.psum_tensor("ps_" + name, shape, dt))
        sem = {}
        for name in sem_names:
            sem[name] = st.enter_context(nc.semaphore(name))
        for eng in ENGS:
            ctx.ss[eng] = st.enter_context(nc.semaphore("ss_" + eng))
        block = st.enter_context(nc.Block())

        raw = [pf(ctx, sb, ps, sem) for pf in phase_fns]
        emitters = []
        for em in raw:
            if isinstance(em, list):
                emitters.extend(em)
            else:
                emitters.append(em)
        bar = st.enter_context(nc.semaphore("stepbar"))
        nbar = len(emitters)

        def run(engname, e):
            for k, em in enumerate(emitters):
                if engname in em:
                    em[engname](e)
                e.engine_nop().then_inc(bar, 1) if hasattr(e, "engine_nop") \
                    else e.nop().then_inc(bar, 1)
                e.wait_ge(bar, 5 * (k + 1))

        @block.sync
        def _(e):
            run("sync", e)

        @block.tensor
        def _(e):
            run("tensor", e)

        @block.vector
        def _(e):
            run("vector", e)

        @block.scalar
        def _(e):
            run("scalar", e)

        @block.gpsimd
        def _(e):
            run("gpsimd", e)

    return nc


def specs_phase_a():
    inputs = {"x": ([T, D], F32), "eyeb": ([128, 128], BF16)}
    outputs = {"xn_fm": INTER["xn_fm"]}
    sbuf = {
        "eyeb": ([128, 128], BF16),
        "a_xt": ([128, 2, D], F32),
        "a_m1": ([128, 2, 4], F32),
        "a_xc2": ([128, 2, D], F32),
        "a_xnb": ([128, 2, D], BF16),
        "a_fmv": ([128, 2, 8, 128], BF16),
    }
    psum = {"a_ptp": ([128, 2, 8, 128], BF16)}
    sems = ["c_done", "a_x", "a_ln", "a_sq", "a_tp", "a_ev", "a_out"]
    return inputs, outputs, sbuf, psum, sems


def host_consts():
    return {"eyeb": np.eye(128, dtype=ml_dtypes.bfloat16)}


def phase_b(ctx, sb, ps, sem):
    aps = ctx.aps
    xn_fm = aps["xn_fm"]
    MM = 49

    wsb = {P: sb["b_w" + P] for P in "qkv"}
    wbw = sb["b_wbw"]
    dcv = {P: sb["b_d" + P] for P in "qkv"}
    xnt = sb["b_xnt"]
    cvb = sb["b_cvb"]
    tails = sb["b_tails"]
    sil = sb["b_sil"]
    sq2 = sb["b_sq2"]
    rin = sb["b_rin"]
    rinb = sb["b_rinb"]
    bet4 = sb["b_bet4"]
    bet4b = sb["b_bet4b"]
    betr = sb["b_betr"]
    betb = sb["b_betb"]
    ofm = {k: sb["b_o" + k] for k in ("q", "k", "kb", "v", "vb")}
    ttm = sb["b_ttm"]
    ones1 = sb["b_ones1"]
    onesc = sb["b_onesc"]

    pp = ps["b_pp"]
    pnrm = ps["b_pnrm"]
    pbeta = ps["b_pbeta"]
    pbc = ps["b_pbc"]

    (s_w, s_xt, s_mm, s_ev, s_sl, s_q2, s_nr, s_ri, s_bt, s_br, s_bc, s_bcc,
     s_sc, s_tp, s_ts, s_st) = (sem[k] for k in (
        "b_w", "b_xt", "b_mm", "b_ev", "b_sl", "b_q2", "b_nr", "b_ri", "b_bt",
        "b_br", "b_bc", "b_bcc", "b_sc", "b_tp", "b_ts", "b_st"))

    PROJ = ("q", "k", "v")

    def Vproj(i, q):
        return i * MM + 1 + 2 * q + 1

    def Vconv(i, q):
        return i * MM + 1 + 2 * q + 2

    def eng_gpsimd(e):
        for P, nm in (("q", "q_w"), ("k", "k_w"), ("v", "v_w")):
            e.dma_start(out=wsb[P][:],
                        in_=aps[nm].rearrange("(db p) f -> p db f", p=128)
                        ).then_inc(s_w, 16)
        e.dma_start(out=wbw[:], in_=aps["b_w"].rearrange("(db p) h -> p db h", p=128)
                    ).then_inc(s_w, 16)
        for P, nm in (("q", "qdiag"), ("k", "kdiag"), ("v", "vdiag")):
            e.dma_start(out=dcv[P][:], in_=aps[nm]).then_inc(s_w, 16)
        e.dma_start(out=ones1[:], in_=aps["ones1b"][0:1, :]).then_inc(s_w, 16)
        e.dma_start(out=sb["b_eps6"][:], in_=aps["c_eps6"]).then_inc(s_w, 16)
        e.dma_start(out=onesc[:], in_=aps["ones_col"]).then_inc(s_w, 16)
        e.memset(tails[:], 0.0)
        e.engine_nop().then_inc(s_w, 1)
        for i in range(NT5):
            c0 = i * 512
            for n, (kind, dst) in enumerate(
                    (("q", "q_fm"), ("k", "k_fm"), ("kb", "kb_fm"), ("vb", None),
                     ("v", "v_fm"))):
                if kind == "vb":
                    continue
                e.wait_ge(s_sc, 4 * i + {"q": 1, "k": 2, "kb": 3}.get(kind, 4))
                if kind == "v":
                    e.wait_ge(s_sl, i * 24 + 24)
                e.dma_start(out=aps[dst][:, :, c0:c0 + 512].rearrange("d p t -> p d t"),
                            in_=ofm[kind][:]).then_inc(s_st, 16)
        e.wait_ge(s_st, 16 * 4 * NT5)

    def eng_sync(e):
        for i in range(NT5):
            c0 = i * 512
            p2 = i % 2
            if i >= 2:
                e.wait_ge(s_mm, (i - 1) * MM)
            e.dma_start(out=xnt[:, p2],
                        in_=xn_fm[:, :, c0:c0 + 512].rearrange("d p t -> p d t")
                        ).then_inc(s_xt, 16)
            e.wait_ge(s_bt, 2 * i + 2)
            for h in range(NH0):
                e.dma_start(out=betr[0:1, h, :], in_=bet4b[h:h + 1, :]).then_inc(s_br, 16)
            e.dma_start(out=aps["beta"][:, c0:c0 + 512], in_=bet4[:]).then_inc(s_br, 16)
            for j in range(4):
                nt = 4 * i + j
                for ki, kind in enumerate(("k", "kb", "vb")):
                    e.wait_ge(s_tp, 16 * 8 * (i * 12 + j * 3 + ki + 1))
                    e.dma_start(
                        out=aps[kind + "_tm"][:, nt, :, :].rearrange(
                            "h p (s f) -> p h s f", s=2),
                        in_=ttm[:, ki].rearrange("p (h s) q -> p h s q", s=2),
                    ).then_inc(s_ts, 16)
        e.wait_ge(s_ts, 16 * 12 * NT5)
        e.wait_ge(s_br, 16 * 5 * NT5)

    def eng_tensor(e):
        e.wait_ge(s_w, 16 * 10 + 1)
        for i in range(NT5):
            p2 = i % 2
            e.wait_ge(s_xt, 16 * (i + 1))
            if i >= 1:
                e.wait_ge(s_bt, 2 * i)
            for db in range(8):
                mm = e.matmul(pbeta[:], wbw[:, db, :], xnt[:, p2, db, :],
                              start=(db == 0), stop=(db == 7))
            mm.then_inc(s_mm, 1)
            for pi, P in enumerate(PROJ):
                for fb in range(8):
                    q = pi * 8 + fb
                    g = i * 24 + q
                    par = g % 2
                    if g >= 2:
                        e.wait_ge(s_sl, g - 1)
                    for db in range(8):
                        mm = e.matmul(pp[:, par], wsb[P][:, db, fb * 128:(fb + 1) * 128],
                                      xnt[:, p2, db, :], start=(db == 0), stop=(db == 7))
                    mm.then_inc(s_mm, 1)
                    e.wait_ge(s_ev, g + 1)
                    for tap in range(4):
                        sft = 3 - tap
                        mm = e.matmul(pp[:, par], dcv[P][:, fb, tap, :],
                                      cvb[:, par, 3 - sft:515 - sft],
                                      start=(tap == 0), stop=(tap == 3))
                    mm.then_inc(s_mm, 1)
                if P in ("q", "k"):
                    off = 0 if P == "q" else 4
                    for h in range(NH0):
                        gn = i * 8 + off + h
                        e.wait_ge(s_q2, i * 16 + 2 * off + 2 * h + 2)
                        if gn >= 2:
                            e.wait_ge(s_ri, gn - 1)
                        for s in range(2):
                            mm = e.matmul(pnrm[:, gn % 2], onesc[:], sq2[:, 2 * h + s, :],
                                          start=(s == 0), stop=(s == 1))
                        mm.then_inc(s_nr, 1)
                        gb = 12 * i + off + h
                        e.wait_ge(s_ri, gn + 1)
                        if gb >= 2:
                            e.wait_ge(s_bcc, gb - 1)
                        e.matmul(pbc[:, gb % 2], ones1[:], rinb[0:1, off + h, :],
                                 start=True, stop=True).then_inc(s_bc, 1)
            e.wait_ge(s_br, 16 * 5 * (i + 1))
            for h in range(NH0):
                gb = 12 * i + 8 + h
                e.wait_ge(s_bcc, gb - 1)
                e.matmul(pbc[:, gb % 2], ones1[:], betr[0:1, h, :],
                         start=True, stop=True).then_inc(s_bc, 1)

    def eng_scalar(e):
        mk = lambda ins: ctx.mark("scalar", ins)
        wt = lambda n: ctx.wt("scalar", e, n)
        for i in range(NT5):
            p2 = i % 2
            e.wait_ge(s_mm, i * MM + 1)
            c1 = mk(e.activation(bet4[:], pbeta[:], AF.Sigmoid))
            wt(c1)
            c2 = mk(e.activation(bet4b[:], bet4[:], AF.Copy))
            wt(c2)
            e.nop().then_inc(s_bt, 2)
            for pi, P in enumerate(PROJ):
                if P == "q" and i >= 1:
                    e.wait_ge(s_nr, i * 8)
                if P == "k":
                    e.wait_ge(s_nr, i * 8 + 4)
                for fb in range(8):
                    q = pi * 8 + fb
                    g = i * 24 + q
                    par = g % 2
                    e.wait_ge(s_mm, Vproj(i, q))
                    if g >= 2:
                        e.wait_ge(s_sl, g - 1)
                    c3 = mk(e.activation(cvb[:, par, 3:515], pp[:, par], AF.Copy))
                    e.copy(cvb[:, par, 0:3], tails[:, q, :])
                    wt(c3)
                    c4 = mk(e.copy(tails[:, q, :], cvb[:, par, 512:515]))
                    wt(c4)
                    e.nop().then_inc(s_ev, 1)
                    e.wait_ge(s_mm, Vconv(i, q))
                    if P == "v":
                        e.activation(ofm["v"][:, fb, :], pp[:, par], AF.Silu
                                     ).then_inc(s_sl, 1)
                    else:
                        c5 = mk(e.activation(sil[:, p2, fb, :] if P == "q"
                                             else sil[:, p2, fb, :],
                                             pp[:, par], AF.Silu))
                        wt(c5)
                        e.square(sq2[:, fb, :], sil[:, p2, fb, :]).then_inc(s_q2, 1)
                        e.nop().then_inc(s_sl, 1)
                if P in ("q", "k"):
                    off = 0 if P == "q" else 4
                    for h in range(NH0):
                        gn = i * 8 + off + h
                        e.wait_ge(s_nr, gn + 1)
                        c6 = mk(e.activation(rin[0:1, 0, :], pnrm[:, gn % 2], AF.Ln,
                                             bias=sb["b_eps6"][0:1, 0:1]))
                        wt(c6)
                        c7 = mk(e.activation(rin[0:1, 1, :], rin[0:1, 0, :], AF.Exp,
                                             scale=-0.5))
                        wt(c7)
                        e.activation(rinb[0:1, off + h, :], rin[0:1, 1, :], AF.Copy
                                     ).then_inc(s_ri, 1)
            for j in range(4):
                for ki, kind in enumerate(("k", "kb", "vb")):
                    e.wait_ge(s_sc, 4 * i + {"k": 2, "kb": 3, "vb": 4}[kind])
                    n_tt = i * 12 + j * 3 + ki
                    if n_tt >= 3:
                        e.wait_ge(s_ts, 16 * (n_tt - 2))
                    for fb in range(8):
                        d = e.dma_start_transpose(ttm[:, ki, fb, :],
                                                  ofm[kind][:, fb, j * 128:(j + 1) * 128])
                        d.then_inc(s_tp, 16)
        e.wait_ge(s_tp, 16 * 24 * NT5)

    def eng_vector(e):
        mk = lambda ins: ctx.mark("vector", ins)
        wt = lambda n: ctx.wt("vector", e, n)
        for i in range(NT5):
            p2 = i % 2
            if i >= 1:
                e.wait_ge(s_st, 16 * 4 * i)
                e.wait_ge(s_tp, 16 * 24 * i)
            for h in range(NH0):
                gb = 12 * i + h
                e.wait_ge(s_bc, gb + 1)
                e.wait_ge(s_sl, i * 24 + 2 * h + 2)
                e.tensor_tensor(ofm["q"][:, 2 * h, :], sil[:, p2, 2 * h, :],
                                pbc[:, gb % 2], ALU.mult)
                e.tensor_tensor(ofm["q"][:, 2 * h + 1, :], sil[:, p2, 2 * h + 1, :],
                                pbc[:, gb % 2], ALU.mult).then_inc(s_bcc, 1)
            e.engine_nop().then_inc(s_sc, 1)
            for h in range(NH0):
                gb = 12 * i + 4 + h
                e.wait_ge(s_bc, gb + 1)
                e.wait_ge(s_sl, i * 24 + 8 + 2 * h + 2)
                e.tensor_tensor(ofm["k"][:, 2 * h, :], sil[:, p2, 2 * h, :],
                                pbc[:, gb % 2], ALU.mult)
                e.tensor_tensor(ofm["k"][:, 2 * h + 1, :], sil[:, p2, 2 * h + 1, :],
                                pbc[:, gb % 2], ALU.mult).then_inc(s_bcc, 1)
            e.engine_nop().then_inc(s_sc, 1)
            for h in range(NH0):
                gb = 12 * i + 8 + h
                e.wait_ge(s_bc, gb + 1)
                e.tensor_copy(betb[:, h, :], pbc[:, gb % 2]).then_inc(s_bcc, 1)
            e.wait_ge(s_bcc, 12 * i + 12)
            for h in range(NH0):
                for s in range(2):
                    tt = e.tensor_tensor(ofm["kb"][:, 2 * h + s, :],
                                         ofm["k"][:, 2 * h + s, :],
                                         betb[:, h, :], ALU.mult)
            tt.then_inc(s_sc, 1)
            e.wait_ge(s_sl, i * 24 + 24)
            for h in range(NH0):
                for s in range(2):
                    tt = e.tensor_tensor(ofm["vb"][:, 2 * h + s, :],
                                         ofm["v"][:, 2 * h + s, :],
                                         betb[:, h, :], ALU.mult)
            tt.then_inc(s_sc, 1)

    return dict(gpsimd=eng_gpsimd, sync=eng_sync, tensor=eng_tensor,
                scalar=eng_scalar, vector=eng_vector)


def specs_phase_b():
    inputs = {
        "xn_fm": INTER["xn_fm"],
        "q_w": ([D, D], F32), "k_w": ([D, D], F32), "v_w": ([D, D], F32),
        "b_w": ([D, NH0], F32),
        "qdiag": ([128, 8, 4, 128], BF16), "kdiag": ([128, 8, 4, 128], BF16),
        "vdiag": ([128, 8, 4, 128], BF16),
        "ones1b": ([128, 128], BF16),
        "ones_col": ([128, 1], BF16),
        "c_eps6": ([1, 1], F32),
    }
    outputs = {k: INTER[k] for k in ("q_fm", "k_fm", "kb_fm", "v_fm",
                                     "k_tm", "kb_tm", "vb_tm", "beta")}
    sbuf = {
        "b_wq": ([128, 8, D], BF16), "b_wk": ([128, 8, D], BF16),
        "b_wv": ([128, 8, D], BF16), "b_wbw": ([128, 8, NH0], BF16),
        "b_dq": ([128, 8, 4, 128], BF16), "b_dk": ([128, 8, 4, 128], BF16),
        "b_dv": ([128, 8, 4, 128], BF16),
        "b_xnt": ([128, 2, 8, 512], BF16),
        "b_cvb": ([128, 2, 515], BF16),
        "b_tails": ([128, 24, 3], BF16),
        "b_sil": ([128, 2, 8, 512], F32),
        "b_sq2": ([128, 8, 512], BF16),
        "b_rin": ([1, 2, 512], F32),
        "b_rinb": ([1, 8, 512], BF16),
        "b_bet4": ([4, 512], F32), "b_bet4b": ([4, 512], BF16),
        "b_betr": ([1, 4, 512], BF16),
        "b_betb": ([128, 4, 512], F32),
        "b_oq": ([128, 8, 512], BF16), "b_ok": ([128, 8, 512], BF16),
        "b_okb": ([128, 8, 512], BF16), "b_ov": ([128, 8, 512], BF16),
        "b_ovb": ([128, 8, 512], BF16),
        "b_ttm": ([128, 3, 8, 128], BF16),
        "b_ones1": ([1, 128], BF16), "b_onesc": ([128, 1], BF16),
        "b_eps6": ([1, 1], F32),
    }
    psum = {
        "b_pp": ([128, 2, 512], F32),
        "b_pnrm": ([1, 2, 512], F32),
        "b_pbeta": ([4, 512], F32),
        "b_pbc": ([128, 2, 512], F32),
    }
    sems = ["b_w", "b_xt", "b_mm", "b_ev", "b_sl", "b_q2", "b_nr", "b_ri",
            "b_bt", "b_br", "b_bc", "b_bcc", "b_sc", "b_tp", "b_ts", "b_st"]
    return inputs, outputs, sbuf, psum, sems


def host_prep_b(z):
    out = {}
    for nm, key in (("q_w", "L0_attn_q_w"), ("k_w", "L0_attn_k_w"),
                    ("v_w", "L0_attn_v_w"), ("b_w", "L0_attn_b_w")):
        out[nm] = np.ascontiguousarray(z[key])
    for nm, key in (("qdiag", "L0_attn_qc_w"), ("kdiag", "L0_attn_kc_w"),
                    ("vdiag", "L0_attn_vc_w")):
        w = np.asarray(z[key]).reshape(D, 4)
        dg = np.zeros((128, 8, 4, 128), np.float32)
        for db in range(8):
            for tap in range(4):
                np.fill_diagonal(dg[:, db, tap, :], w[db * 128:(db + 1) * 128, tap])
        out[nm] = dg.astype(ml_dtypes.bfloat16)
    out["ones1b"] = np.ones((128, 128), dtype=ml_dtypes.bfloat16)
    out["ones_col"] = np.ones((128, 1), dtype=ml_dtypes.bfloat16)
    out["c_eps6"] = np.full((1, 1), 1e-6, np.float32)
    return out


def phase_c(ctx, sb, ps, sem):
    aps = ctx.aps
    NG = NT1
    qfD, kfD, kbD = aps["q_fm"], aps["k_fm"], aps["kb_fm"]
    ktmD, kbtmD, vbtmD = aps["k_tm"], aps["kb_tm"], aps["vb_tm"]
    dfmD = aps["delta_fm"]

    qf = sb["c_qf"]
    ktm = sb["c_ktm"]
    Ub = sb["c_U"]
    Wb = sb["c_W"]
    Gb = sb["c_G"]
    Hb = sb["c_H"]
    Bb = sb["c_B"]
    uT = sb["c_u"]
    wT = sb["c_wT"]
    aT = sb["c_aT"]
    uh = sb["c_uh"]
    Ssb = sb["c_S"]
    Sbf = sb["c_Sb"]
    dst = sb["c_dst"]
    eyeb = sb["eyeb"]
    mup = sb["c_mup"]
    mlo = sb["c_mlo"]
    minc = sb["c_minc"]

    pA = ps["c_pA"]; pB = ps["c_pB"]; pC = ps["c_pC"]
    pD = ps["c_pD"]; pE = ps["c_pE"]; pF = ps["c_pF"]; pG2 = ps["c_pG2"]

    (s_ld, s_tm, s_ve, s_pr, s_sa, s_sb2, s_sc2, s_sf, s_oev, s_ost, s_cst
     ) = (sem[k] for k in ("c_ld", "c_tm", "c_ve", "c_pr", "c_sa", "c_sb2",
                           "c_sc2", "c_sf", "c_oev", "c_ost", "c_cst"))
    cnt = {"tm": 0, "ve": 0, "pr": 0, "sa": 0, "sb2": 0, "sc2": 0, "sf": 0,
           "oev": 0, "ost": 0, "ld": 0, "cst": 0}
    OFF_A = 4 + 8 * NDBL
    TM_G = 4 + 8 * NDBL + 4
    VE_G = 4 + 8 * NDBL + 4

    def eng_sync(e):
        for g in range(NG):
            par = g % 2
            if g >= 2:
                e.wait_ge(s_sf, 4 * (g - 1))
                e.wait_ge(s_oev, 4 * (g - 1))
            for h in range(NH0):
                e.dma_start(out=qf[:, h, par], in_=qfD[2 * h:2 * h + 2, :, g * 128:(g + 1) * 128]
                            .rearrange("s p t -> p s t")).then_inc(s_ld, 16)
                e.dma_start(out=ktm[:, h, par], in_=ktmD[h, g]).then_inc(s_ld, 16)
            if g >= 1:
                for h in range(NH0):
                    e.wait_ge(s_oev, 4 * (g - 1) + h + 1)
                    e.dma_start(out=dfmD[2 * h:2 * h + 2, :, (g - 1) * 128:g * 128]
                                .rearrange("s p t -> p s t"),
                                in_=dst[:, h, :, :]).then_inc(s_ost, 16)
        for h in range(NH0):
            e.wait_ge(s_oev, 4 * (NG - 1) + h + 1)
            e.dma_start(out=dfmD[2 * h:2 * h + 2, :, (NG - 1) * 128:NG * 128]
                        .rearrange("s p t -> p s t"),
                        in_=dst[:, h, :, :]).then_inc(s_ost, 16)
        if "dbg_c" in aps:
            e.wait_ge(sem["c_dbg"], 1)
            e.dma_start(out=aps["dbg_c"], in_=sb["c_dbg"][:]).then_inc(s_ost, 16)
            e.wait_ge(s_ost, 16 * 4 * NG + 16)
        e.wait_ge(s_ost, 16 * 4 * NG)

    kf4 = sb["c_kf4"]
    kbf4 = sb["c_kbf4"]
    kbtm4 = sb["c_kbtm4"]
    vbtm4 = sb["c_vbtm4"]

    def eng_gpsimd2(e):
        if "dbg_c" in aps:
            pass
        e.dma_start(out=mup[:], in_=aps["m_upneg"]).then_inc(s_ld, 16)
        e.dma_start(out=mlo[:], in_=aps["m_loneg"]).then_inc(s_ld, 16)
        e.dma_start(out=minc[:], in_=aps["m_incu"]).then_inc(s_ld, 16)
        for g in range(NG):
            par = g % 2
            if g >= 2:
                e.wait_ge(s_sf, 4 * (g - 1))
            for h in range(NH0):
                e.dma_start(out=kf4[:, h, par],
                            in_=kfD[2 * h:2 * h + 2, :, g * 128:(g + 1) * 128]
                            .rearrange("s p t -> p s t")).then_inc(s_ld, 16)
                e.dma_start(out=kbf4[:, h, par],
                            in_=kbD[2 * h:2 * h + 2, :, g * 128:(g + 1) * 128]
                            .rearrange("s p t -> p s t")).then_inc(s_ld, 16)
                e.dma_start(out=kbtm4[:, h, par], in_=kbtmD[h, g]).then_inc(s_ld, 16)
                e.dma_start(out=vbtm4[:, h, par], in_=vbtmD[h, g]).then_inc(s_ld, 16)

    LD_G = 6 * NH0

    def eng_tensor(e):
        for g in range(NG):
            par = g % 2
            e.wait_ge(s_ld, 16 * (3 + LD_G * (g + 1)))
            if g >= 2:
                e.wait_ge(s_ve, VE_G * (g - 1))
            for h in range(NH0):
                for s in range(2):
                    e.matmul(pA[:, 0:128], kf4[:, h, par, s, :], kbf4[:, h, par, s, :],
                             start=(s == 0), stop=(s == 1))
                for s in range(2):
                    mm = e.matmul(pB[:, 0:128], kbf4[:, h, par, s, :], kf4[:, h, par, s, :],
                                  start=(s == 0), stop=(s == 1))
                mm.then_inc(s_tm, 1)
                e.wait_ge(s_ve, VE_G * g + h + 1)
            for j in range(NDBL):
                for h in range(NH0):
                    e.matmul(pA[:, 0:128], Wb[:, h, :], Ub[:, h, :], start=True, stop=True)
                    mm = e.matmul(pB[:, 0:128], Ub[:, h, :], Wb[:, h, :], start=True, stop=True)
                    mm.then_inc(s_tm, 1)
                    e.wait_ge(s_ve, VE_G * g + 4 + (8 * j) + 2 * h + 1)
                    e.matmul(pA[:, 0:128], Hb[:, h, :], Bb[:, h, :], start=True, stop=True)
                    mm = e.matmul(pB[:, 0:128], Bb[:, h, :], Hb[:, h, :], start=True, stop=True)
                    mm.then_inc(s_tm, 1)
                    e.wait_ge(s_ve, VE_G * g + 4 + (8 * j) + 2 * h + 2)
            for h in range(NH0):
                for s in range(2):
                    mm = e.matmul(pA[:, 0:128], kf4[:, h, par, s, :], qf[:, h, par, s, :],
                                  start=(s == 0), stop=(s == 1))
                e.matmul(pC[:, 0:256], Gb[:, h, :], vbtm4[:, h, par], start=True, stop=True)
                for fsub in range(2):
                    mm2 = e.matmul((pD if fsub == 0 else pE)[:, 0:128],
                                   kbtm4[:, h, par, fsub * 128:(fsub + 1) * 128],
                                   Gb[:, h, :], start=True, stop=True)
                mm2.then_inc(s_tm, 1)
                e.wait_ge(s_ve, VE_G * g + 4 + 8 * NDBL + h + 1)
            for h in range(NH0):
                e.wait_ge(s_cst, 4 * g + h + 1)
                for s in range(2):
                    mm = e.matmul(pC[:, 0:256], wT[:, h, par, s, :], Sbf[:, h, s, :],
                                  start=(s == 0), stop=(s == 1))
                mm.then_inc(s_sa, 1)
                e.wait_ge(s_sb2, 4 * g + h + 1)
                for eb in range(2):
                    pX = pD if eb == 0 else pE
                    for s in range(2):
                        e.matmul(pX[:, 0:128], Sbf[:, h, s, eb * 128:(eb + 1) * 128],
                                 qf[:, h, par, s, :], start=(s == 0), stop=False)
                    mm = e.matmul(pX[:, 0:128], uh[:, h, eb * 128:(eb + 1) * 128],
                                  aT[:, h, par, :], start=False, stop=True)
                mm.then_inc(s_sc2, 1)
                for fsub in range(2):
                    mm = e.matmul((pF if fsub == 0 else pG2)[:, 0:256],
                                  ktm[:, h, par, fsub * 128:(fsub + 1) * 128],
                                  uh[:, h, :], start=True, stop=True)
                mm.then_inc(s_sc2, 1)
                e.wait_ge(s_oev, 4 * g + h + 1)
                e.wait_ge(s_sf, 4 * g + h + 1)

    def eng_vector(e):
        mkv = lambda ins: ctx.mark("vector", ins)
        wtv = lambda n: ctx.wt("vector", e, n)
        dbg = sb.get("c_dbg")
        e.memset(Ssb[:], 0.0)
        cm = ctx.mark("vector", e.memset(Sbf[:], 0.0))
        e.memset(uh[:], 0.0)
        ctx.wt("vector", e, cm)
        for _ in range(4):
            e.engine_nop().then_inc(s_cst, 1)
        for g in range(NG):
            par = g % 2
            for h in range(NH0):
                e.wait_ge(s_tm, TM_G * g + h + 1)
                e.tensor_tensor(Ub[:, h, :], pA[:, 0:128], mup[:], ALU.mult)
                c = mkv(e.tensor_tensor(Wb[:, h, :], pB[:, 0:128], mlo[:], ALU.mult))
                wtv(c)
                e.tensor_tensor(Gb[:, h, :], Ub[:, h, :], eyeb[:], ALU.add)
                e.tensor_tensor(Hb[:, h, :], Wb[:, h, :], eyeb[:], ALU.add
                                ).then_inc(s_ve, 1)
                if dbg is not None and g == 0 and h == 0:
                    e.tensor_copy(dbg[:, 0, :], Ub[:, 0, :])
                    e.tensor_copy(dbg[:, 1, :], Wb[:, 0, :])
            for j in range(NDBL):
                for h in range(NH0):
                    e.wait_ge(s_tm, TM_G * g + 4 + 8 * j + 2 * h + 1)
                    c1 = mkv(e.tensor_tensor(Bb[:, h, :], pA[:, 0:128], eyeb[:], ALU.add))
                    e.tensor_copy(Ub[:, h, :], pA[:, 0:128])
                    c2 = mkv(e.tensor_copy(Wb[:, h, :], pB[:, 0:128]))
                    wtv(c1)
                    e.engine_nop().then_inc(s_ve, 1)
                    e.wait_ge(s_tm, TM_G * g + 4 + 8 * j + 2 * h + 2)
                    e.tensor_copy(Gb[:, h, :], pA[:, 0:128])
                    c3 = mkv(e.tensor_copy(Hb[:, h, :], pB[:, 0:128]))
                    wtv(c3)
                    e.engine_nop().then_inc(s_ve, 1)
            if dbg is not None and g == 0:
                e.tensor_copy(dbg[:, 2, :], Gb[:, 0, :])
                e.tensor_copy(dbg[:, 3, :], Hb[:, 0, :])
            for h in range(NH0):
                e.wait_ge(s_tm, TM_G * g + 4 + 8 * NDBL + h + 1)
                e.tensor_tensor(aT[:, h, par, :], pA[:, 0:128], minc[:], ALU.mult)
                e.tensor_copy(uT[:, h, par], pC[:, 0:256])
                for fsub in range(2):
                    cc = mkv(e.tensor_copy(wT[:, h, par, fsub, :],
                                           (pD if fsub == 0 else pE)[:, 0:128]))
                wtv(cc)
                e.engine_nop().then_inc(s_ve, 1)
                e.engine_nop().then_inc(s_pr, 1)
            for h in range(NH0):
                e.wait_ge(s_sa, 4 * g + h + 1)
                e.tensor_tensor(uh[:, h, :], uT[:, h, par], pC[:, 0:256], ALU.subtract
                                ).then_inc(s_sb2, 1)
                if dbg is not None and g == 0 and h == 0:
                    e.tensor_copy(dbg[:, 4, :], uT[:, 0, 0, 0:128])
                    e.tensor_copy(dbg[:, 5, :], aT[:, 0, 0, :])
                    cdbg = mkv(e.tensor_copy(dbg[:, 6, :], uh[:, 0, 0:128]))
                    wtv(cdbg)
                    e.engine_nop().then_inc(sem["c_dbg"], 1)
                e.wait_ge(s_sc2, 8 * g + 2 * h + 2)
                for s2 in range(2):
                    cs = mkv(e.tensor_tensor(Ssb[:, h, s2, :], Ssb[:, h, s2, :],
                                             (pF if s2 == 0 else pG2)[:, 0:256], ALU.add))
                wtv(cs)
                e.nop().then_inc(s_sf, 1) if False else None
                e.engine_nop().then_inc(s_sf, 1)

    def eng_scalar(e):
        mks = lambda ins: ctx.mark("scalar", ins)
        wts = lambda n: ctx.wt("scalar", e, n)
        for g in range(NG):
            par = g % 2
            for h in range(NH0):
                e.wait_ge(s_sc2, 8 * g + 2 * h + 1)
                if g >= 1:
                    e.wait_ge(s_ost, 16 * (4 * (g - 1) + h + 1))
                e.activation(dst[:, h, 0, :], pD[:, 0:128], AF.Copy)
                cpy = mks(e.activation(dst[:, h, 1, :], pE[:, 0:128], AF.Copy))
                wts(cpy)
                e.nop().then_inc(s_oev, 1)
                e.wait_ge(s_sf, 4 * g + h + 1)
                c2 = mks(e.activation(Sbf[:, h, :, :].rearrange("p s t -> p (s t)"),
                                      Ssb[:, h, :, :].rearrange("p s t -> p (s t)"),
                                      AF.Copy))
                wts(c2)
                e.nop().then_inc(s_cst, 1)

    return dict(sync=eng_sync, gpsimd=eng_gpsimd2, tensor=eng_tensor,
                vector=eng_vector, scalar=eng_scalar)


def specs_phase_c():
    inputs = {k: INTER[k] for k in ("q_fm", "k_fm", "kb_fm", "k_tm", "kb_tm",
                                    "vb_tm")}
    inputs.update({
        "eyeb": ([128, 128], BF16),
        "m_upneg": ([128, 128], F32), "m_loneg": ([128, 128], F32),
        "m_incu": ([128, 128], BF16),
    })
    outputs = {"delta_fm": ([8, 128, T], BF16)}
    sbuf = {
        "eyeb": ([128, 128], BF16),
        "c_qf": ([128, 4, 2, 2, 128], BF16),
        "c_ktm": ([128, 4, 2, 256], BF16),
        "c_kf4": ([128, 4, 2, 2, 128], BF16),
        "c_kbf4": ([128, 4, 2, 2, 128], BF16),
        "c_kbtm4": ([128, 4, 2, 256], BF16),
        "c_vbtm4": ([128, 4, 2, 256], BF16),
        "c_U": ([128, 4, 128], BF16), "c_W": ([128, 4, 128], BF16),
        "c_G": ([128, 4, 128], BF16), "c_H": ([128, 4, 128], BF16),
        "c_B": ([128, 4, 128], BF16),
        "c_u": ([128, 4, 2, 256], BF16),
        "c_wT": ([128, 4, 2, 2, 128], BF16),
        "c_aT": ([128, 4, 2, 128], BF16),
        "c_uh": ([128, 4, 256], BF16),
        "c_S": ([128, 4, 2, 256], F32),
        "c_Sb": ([128, 4, 2, 256], BF16),
        "c_dst": ([128, 4, 2, 128], BF16),
        "c_mup": ([128, 128], F32), "c_mlo": ([128, 128], F32),
        "c_minc": ([128, 128], BF16),
    }
    psum = {
        "c_pA": ([128, 512], F32), "c_pB": ([128, 512], F32),
        "c_pC": ([128, 512], F32), "c_pD": ([128, 512], F32),
        "c_pE": ([128, 512], F32), "c_pF": ([128, 512], F32),
        "c_pG2": ([128, 512], F32),
    }
    sems = ["c_ld", "c_tm", "c_ve", "c_pr", "c_sa", "c_sb2", "c_sc2", "c_sf",
            "c_oev", "c_ost", "c_cst", "c_dbg"]
    return inputs, outputs, sbuf, psum, sems


def host_masks():
    iu = np.triu(np.ones((128, 128), np.float32), 1)
    il = np.tril(np.ones((128, 128), np.float32), -1)
    return {
        "m_upneg": -iu, "m_loneg": -il,
        "m_incu": np.triu(np.ones((128, 128))).astype(ml_dtypes.bfloat16),
    }


FIRK = (1, 3, 7, 31)

def phase_d(ctx, sb, ps, sem):
    aps = ctx.aps
    vfmD = aps["v_fm"]
    outD = [aps["fir%d_fm" % k] for k in FIRK]
    dg = sb["d_dg"]
    vw = sb["d_vw"]
    ob = sb["d_ob"]
    pF = ps["d_pF"]
    s_dg, s_vw, s_fm, s_ev, s_st, s_dg1, s_st1 = (
        sem[k] for k in ("d_dg", "d_vw", "d_fm", "d_ev", "d_st", "d_dg1",
                         "d_st1"))
    iters = [(fb, bi) for fb in range(8) for bi in range(4)]

    def eng_sync(e):
        for n, (fb, bi) in enumerate(iters):
            for i in range(NT5):
                m = n * NT5 + i
                par = m % 2
                if m >= 2:
                    e.wait_ge(s_fm, m - 1)
                c0 = i * 512
                if i == 0:
                    e.dma_start(out=vw[:, par, 0:30],
                                in_=aps["zpad30"]).then_inc(s_vw, 16)
                    e.dma_start(out=vw[:, par, 30:542],
                                in_=vfmD[fb:fb + 1, :, 0:512].rearrange("o p t -> p (o t)")
                                ).then_inc(s_vw, 16)
                else:
                    e.dma_start(out=vw[:, par, 0:542],
                                in_=vfmD[fb:fb + 1, :, c0 - 30:c0 + 512].rearrange("o p t -> p (o t)")
                                ).then_inc(s_vw, 16)

    def eng_tensor(e):
        for n, (fb, bi) in enumerate(iters):
            pdg = n % 2
            kk = FIRK[bi]
            e.wait_ge(s_dg1 if n % 2 else s_dg, 16 * (n // 2 + 1))
            for i in range(NT5):
                m = n * NT5 + i
                par = m % 2
                e.wait_ge(s_vw, 16 * (n * 17 + i + 2))
                if m >= 2:
                    e.wait_ge(s_ev, m - 1)
                for tap in range(kk):
                    sft = kk - 1 - tap
                    mm = e.matmul(pF[:, par], dg[:, pdg, tap, :],
                                  vw[:, par, 30 - sft:542 - sft],
                                  start=(tap == 0), stop=(tap == kk - 1))
                mm.then_inc(s_fm, 1)

    def eng_scalar(e):
        for n, (fb, bi) in enumerate(iters):
            for i in range(NT5):
                m = n * NT5 + i
                par = m % 2
                e.wait_ge(s_fm, m + 1)
                e.wait_ge(s_st1 if m % 2 else s_st, 16 * (m // 2 + 1))
                e.activation(ob[:, par], pF[:, par], AF.Copy).then_inc(s_ev, 1)

    def eng_gpsimd2(e):
        e.sem_inc(s_st, 16)
        e.sem_inc(s_st1, 16)
        for n, (fb, bi) in enumerate(iters):
            par = n % 2
            if n >= 2:
                e.wait_ge(s_fm, (n - 1) * NT5)
            e.dma_start(out=dg[:, par, 0:FIRK[bi], :],
                        in_=aps["fdiag%d" % FIRK[bi]][:, fb, :, :]
                        ).then_inc(s_dg1 if par else s_dg, 16)
            for i in range(NT5):
                m = n * NT5 + i
                e.wait_ge(s_ev, m + 1)
                e.dma_start(out=outD[bi][fb:fb + 1, :, i * 512:(i + 1) * 512]
                            .rearrange("o p t -> p (o t)"),
                            in_=ob[:, m % 2]).then_inc(s_st1 if m % 2 else s_st,
                                                       16)
        e.wait_ge(s_st, 16 + 16 * 16 * NT5)
        e.wait_ge(s_st1, 16 + 16 * 16 * NT5)

    return dict(gpsimd=eng_gpsimd2, sync=eng_sync, tensor=eng_tensor,
                scalar=eng_scalar)


def specs_phase_d():
    inputs = {"v_fm": INTER["v_fm"], "zpad30": ([128, 30], BF16)}
    for k in FIRK:
        inputs["fdiag%d" % k] = ([128, 8, k, 128], BF16)
    outputs = {"fir%d_fm" % k: ([8, 128, T], BF16) for k in FIRK}
    sbuf = {
        "d_dg": ([128, 2, 31, 128], BF16),
        "d_vw": ([128, 2, 542], BF16),
        "d_ob": ([128, 2, 512], BF16),
    }
    psum = {"d_pF": ([128, 2, 512], F32)}
    sems = ["d_dg", "d_vw", "d_fm", "d_ev", "d_st", "d_dg1", "d_st1"]
    return inputs, outputs, sbuf, psum, sems


def host_prep_d(z):
    out = {}
    for k, key in zip(FIRK, ("L0_attn_fir_w1", "L0_attn_fir_w3",
                             "L0_attn_fir_w7", "L0_attn_fir_w31")):
        w = np.asarray(z[key]).reshape(D, k)
        dgm = np.zeros((128, 8, k, 128), np.float32)
        for fbi in range(8):
            for tap in range(k):
                np.fill_diagonal(dgm[:, fbi, tap, :], w[fbi * 128:(fbi + 1) * 128, tap])
        out["fdiag%d" % k] = dgm.astype(ml_dtypes.bfloat16)
    out["zpad30"] = np.zeros((128, 30), ml_dtypes.bfloat16)
    return out




BRN = 6

def phase_eg(ctx, sb, ps, sem):
    aps = ctx.aps
    brD = [aps["fir%d_fm" % k] for k in FIRK] + [aps["delta_fm"], aps["v_fm"]]
    xnD, xD, y0D, srD = aps["xn_fm"], aps["x"], aps["y0"], aps["stats_raw"]
    B = lambda n: sb[n]
    (brt, xnt, xt, x2, ax, stage, sder, stmp, hgate, wgt, wgtb, smx,
     exw, wTt, wbc, facc, rst4, rstb4, fusedb, y0sb, w1x, w1s, w2s, ows,
     onc, b1c, selb, on1f, b2r, glt, b2bc, invtbc, eps5, onec, onescb,
     ones1b2, eyeb) = (B(n) for n in (
        "g_brt", "g_xnt", "g_xt", "g_x2", "g_ax", "g_stage", "g_sder",
        "g_stmp", "g_hgate", "g_wgt", "g_wgtb", "g_smx", "g_exw",
        "g_wTt", "g_wbc", "g_facc", "g_rst4", "g_rstb4", "g_fusedb",
        "g_y0sb", "g_w1x", "g_w1s", "g_w2s", "g_ows", "g_onc", "g_b1c",
        "g_selb", "g_on1f", "g_b2r", "g_glt", "g_b2bc", "g_invtbc", "g_eps5",
        "g_onec", "g_onescb", "g_ones1b2", "eyeb"))
    statsb = sder
    f2 = B("g_x2")
    stmp2 = B("g_stmp2")
    p45, p67, pt, po = ps["g_p45"], ps["g_p67"], ps["g_pt"], ps["g_po"]
    S = lambda n: sem[n]
    (s_w, s_ld, s_sm, s_se, s_so, s_si, s_px, s_pa, s_d1, s_d2, s_d3, s_hg,
     s_he, s_lg, s_v1, s_s1, s_v2, s_s2, s_t2, s_v3, s_bc, s_be, s_cv, s_cp,
     s_r1, s_r2, s_rb, s_om, s_y0v, s_oe, s_i1, s_i2, s_i3) = (S(n) for n in (
        "g_w", "g_ld", "g_sm", "g_se", "g_so", "g_si", "g_px", "g_pa", "g_d1",
        "g_d2", "g_d3", "g_hg", "g_he", "g_lg", "g_v1", "g_s1", "g_v2",
        "g_s2", "g_t2", "g_v3", "g_bc", "g_be", "g_cv", "g_cp", "g_r1",
        "g_r2", "g_rb", "g_om", "g_y0v", "g_oe", "g_i1", "g_i2", "g_i3"))

    steps = []

    def init_g(e):
        for nm, dst, rr in (
            ("w1x", w1x, "(db p) f -> p db f"), ("w1s", w1s, None),
            ("mlp_w2", w2s, "(db p) f -> p db f"), ("o_w", ows, "(db p) f -> p db f"),
            ("onormc", onc, None), ("b1c", b1c, None), ("selmat", selb, None),
            ("ones1f", on1f, None), ("c_eps5", eps5, None), ("c_one", onec, None),
            ("ones_col", onescb, None),
        ):
            src = aps[nm].rearrange(rr, p=128) if rr else aps[nm]
            e.dma_start(out=dst[:], in_=src).then_inc(s_w, 16)
        e.dma_start(out=b2r[:], in_=aps["mlp_b2"][None, :]).then_inc(s_w, 16)
        e.dma_start(out=glt[:], in_=aps["gate_log_temp"][None, :]).then_inc(s_w, 16)
        e.dma_start(out=ones1b2[:], in_=aps["ones1b"][0:1, :]).then_inc(s_w, 16)
        e.wait_ge(s_w, 16 * 14)
    steps.append(dict(gpsimd=init_g))

    def i2_v(e):
        mkv = lambda x: ctx.mark("vector", x)
        wtv = lambda n: ctx.wt("vector", e, n)
        e.memset(sder[:], 0.0)
        c1 = mkv(e.tensor_scalar_max(wgt[0:1, 0:4], glt[:], 0.0))
        c1b = mkv(e.tensor_scalar_min(wgt[0:1, 4:8], glt[:], 0.0))
        wtv(c1b)
        c1c = mkv(e.tensor_tensor(wgt[0:1, 8:12], wgt[0:1, 4:8], wgt[0:1, 0:4],
                                  ALU.subtract))
        wtv(c1c)
        e.engine_nop().then_inc(s_i1, 1)
        e.wait_ge(s_i2, 1)
        c3 = mkv(e.tensor_tensor(wgt[0:1, 12:16], wgt[0:1, 12:16], wgt[0:1, 0:4], ALU.add))
        wtv(c3)
        c4 = mkv(e.tensor_scalar(wgt[0:1, 16:20], wgt[0:1, 12:16], 0.5, None, ALU.add))
        wtv(c4)
        c5 = mkv(e.reciprocal(wgt[0:1, 20:24], wgt[0:1, 16:20]))
        wtv(c5)
        for h in range(NH0):
            c6 = mkv(e.tensor_scalar(exw[0:1, 6 * h:6 * h + 6], on1f[0:1, 0:6],
                                     wgt[0:1, 20 + h:21 + h], None, ALU.mult))
        wtv(c6)
        e.engine_nop().then_inc(s_i1, 1)
        e.wait_ge(s_i3, 2)
        e.tensor_copy(b2bc[:], p45[:, 0, 0:24])
        cc = mkv(e.tensor_copy(invtbc[:], p45[:, 1, 0:24]))
        wtv(cc)

    def i2_s(e):
        mks = lambda x: ctx.mark("scalar", x)
        wts = lambda n: ctx.wt("scalar", e, n)
        e.wait_ge(s_i1, 1)
        c1 = mks(e.activation(wgt[0:1, 12:16], wgt[0:1, 8:12], AF.Exp))
        wts(c1)
        c2 = mks(e.activation(wgt[0:1, 12:16], wgt[0:1, 12:16], AF.Ln,
                              bias=onec[0:1, 0:1]))
        wts(c2)
        e.nop().then_inc(s_i2, 1)

    def i2_t(e):
        e.matmul(p45[:, 0, 0:24], on1f[:], b2r[:], start=True, stop=True
                 ).then_inc(s_i3, 1)
        e.wait_ge(s_i1, 2)
        e.matmul(p45[:, 1, 0:24], on1f[:], exw[0:1, :], start=True, stop=True
                 ).then_inc(s_i3, 1)
    steps.append(dict(vector=i2_v, scalar=i2_s, tensor=i2_t))

    for i in range(NT5):
        par = i % 2
        c0 = i * 512

        def s1_g(e, i=i, par=par, c0=c0):
            for b in range(BRN):
                e.dma_start(out=brt[:, b], in_=brD[b][:, :, c0:c0 + 512]
                            .rearrange("d p t -> p d t")).then_inc(s_ld, 16)
            e.dma_start(out=xnt[:], in_=xnD[:, :, c0:c0 + 512]
                        .rearrange("d p t -> p d t")).then_inc(s_ld, 16)
            e.wait_ge(s_ld, 16 * 7 * (i + 1))
        tile_steps = []
        tile_steps.append(dict(gpsimd=s1_g))

        def s2_t(e, i=i, par=par):
            for b in range(BRN):
                for v3 in range(3):
                    if v3 == 1:
                        e.wait_ge(s_px, i * BRN + b + 1)
                    if v3 == 2:
                        e.wait_ge(s_pa, i * BRN + b + 1)
                    for h in range(NH0):
                        gidx = i * 72 + b * 12 + v3 * 4 + h
                        if gidx >= 2:
                            e.wait_ge(s_se, gidx - 1)
                        for s2 in range(2):
                            st = (brt[:, b, 2 * h + s2, :] if v3 == 0 else
                                  (x2[:, 2 * h + s2, :] if v3 == 1 else
                                   ax[:, 2 * h + s2, :]))
                            mm = e.matmul(p45[0:1, gidx % 2, 0:512], onescb[:], st,
                                          start=(s2 == 0), stop=(s2 == 1))
                        mm.then_inc(s_sm, 1)

        def s2_s(e, i=i, par=par):
            for b in range(BRN):
                if i * BRN + b >= 1:
                    e.wait_ge(s_sm, (i * BRN + b - 1) * 12 + 12)
                cq = ctx.mark("scalar", e.square(x2[:].rearrange("p a b -> p (a b)"),
                         brt[:, b].rearrange("p a b -> p (a b)")))
                e.nop().then_inc(s_px, 1)
                ctx.wt("scalar", e, cq)
                e.activation(ax[:].rearrange("p a b -> p (a b)"),
                             x2[:].rearrange("p a b -> p (a b)"), AF.Sqrt
                             ).then_inc(s_pa, 1)
                if i * BRN + b >= 1:
                    e.wait_ge(s_so, 16 * (i * BRN + b))
                for n12 in range(12):
                    gidx = i * 72 + b * 12 + n12
                    e.wait_ge(s_sm, gidx + 1)
                    e.activation(stage[:, n12, :],
                                 p45[0:1, gidx % 2, 0:512], AF.Copy).then_inc(s_se, 1)

        def s2_v(e, i=i, par=par):
            pass

        def s2_sy(e, i=i, c0=c0):
            for b in range(BRN):
                e.wait_ge(s_se, i * 72 + (b + 1) * 12)
                e.dma_start(out=srD[b, :, :, c0:c0 + 512],
                            in_=stage[:].rearrange(
                                "o (v h) t -> o v h t", v=3)
                            ).then_inc(s_so, 16)
            e.wait_ge(s_so, 16 * BRN * (i + 1))
        tile_steps.append(dict(tensor=s2_t, scalar=s2_s, sync=s2_sy))

        def s3_sy(e, i=i, c0=c0):
            for v3 in range(3):
                e.dma_start(out=sder[32 * v3:32 * v3 + 24, :],
                            in_=srD[:, v3, :, c0:c0 + 512]).then_inc(s_si, 16)
            e.wait_ge(s_si, 16 * 3 * (i + 1))
        tile_steps.append(dict(sync=s3_sy))

        def s4_s(e, i=i):
            mks = lambda x: ctx.mark("scalar", x)
            wts = lambda n: ctx.wt("scalar", e, n)
            c1 = mks(e.activation(sder[96:120, :], sder[32:56, :], AF.Sqrt))
            c2 = mks(e.square(stmp[:], sder[0:24, :]))
            wts(c2)
            e.nop().then_inc(s_d1, 1)
            e.wait_ge(s_d2, i + 1)
            c3 = mks(e.activation(sder[32:56, :], stmp[:], AF.Sqrt,
                                  scale=1.0 / 255.0))
            wts(c3)
            e.wait_ge(s_d3, i + 1)

        def s4_v(e, i=i):
            mkv = lambda x: ctx.mark("vector", x)
            wtv = lambda n: ctx.wt("vector", e, n)
            e.wait_ge(s_d1, i + 1)
            c0m = mkv(e.tensor_copy(stmp2[:], sder[32:56, :]))
            wtv(c0m)
            c1 = mkv(e.scalar_tensor_tensor(stmp[:], stmp[:], -1.0 / 256.0,
                                            stmp2[:], ALU.mult, ALU.add))
            wtv(c1)
            c2 = mkv(e.tensor_scalar_max(stmp[:], stmp[:], 0.0))
            wtv(c2)
            e.engine_nop().then_inc(s_d2, 1)
            c3 = mkv(e.tensor_scalar_mul(sder[0:24, :], sder[0:24, :], 1.0 / 256.0))
            c4 = mkv(e.tensor_scalar_mul(sder[64:88, :], sder[64:88, :], 1.0 / 256.0))
            wtv(c4)
            e.engine_nop().then_inc(s_d3, 1)
        tile_steps.append(dict(scalar=s4_s, vector=s4_v))

        def s5_t(e, i=i, par=par):
            for hgb in range(8):
                gh = i * 8 + hgb
                if gh >= 2:
                    e.wait_ge(s_he, gh - 1)
                for db in range(8):
                    e.matmul(p67[:, gh % 2], w1x[:, db, hgb * 128:(hgb + 1) * 128],
                             xnt[:, db], start=(db == 0), stop=False)
                e.matmul(p67[:, gh % 2], w1s[:, hgb * 128:(hgb + 1) * 128],
                         statsb[:], start=False, stop=True).then_inc(s_hg, 1)

        def s5_s(e, i=i):
            for hgb in range(8):
                gh = i * 8 + hgb
                e.wait_ge(s_hg, gh + 1)
                e.activation(hgate[:, hgb, :], p67[:, gh % 2], AF.Gelu,
                             bias=b1c[:, hgb:hgb + 1]).then_inc(s_he, 1)
        tile_steps.append(dict(tensor=s5_t, scalar=s5_s))

        def s6_t(e, i=i):
            for j in range(4):
                gl = i * 4 + j
                if gl >= 2:
                    e.wait_ge(s_v1, gl - 1)
                for hgb in range(8):
                    mm = e.matmul(p45[:, gl % 2, 0:24],
                                  hgate[:, hgb, j * 128:(j + 1) * 128],
                                  w2s[:, hgb, :], start=(hgb == 0), stop=(hgb == 7))
                mm.then_inc(s_lg, 1)
                e.wait_ge(s_s2, gl + 1)
                e.transpose(pt[0:24, j * 128:(j + 1) * 128], wgtb[:, 0:24],
                            eyeb[:]).then_inc(s_t2, 1)

        def s6_v(e, i=i):
            mkv = lambda x: ctx.mark("vector", x)
            wtv = lambda n: ctx.wt("vector", e, n)
            import concourse.bass as bass_m
            for j in range(4):
                gl = i * 4 + j
                e.wait_ge(s_lg, gl + 1)
                c1 = mkv(e.tensor_tensor(exw[:], p45[:, gl % 2, 0:24], b2bc[:],
                                         ALU.add))
                wtv(c1)
                e.engine_nop().then_inc(s_v1, 1)
                c2 = mkv(e.tensor_tensor(exw[:], exw[:], invtbc[:], ALU.mult))
                wtv(c2)
                e3 = exw[:].rearrange("p (h s) -> p h s", h=4)
                c3 = mkv(e.reduce_max(smx[:, :, 0:1], e3, axis=AX.X))
                wtv(c3)
                a1, a2 = bass_m.broadcast_tensor_aps(e3, smx[:, :, 0:1])
                c4 = mkv(e.tensor_tensor(e3, a1, a2, ALU.subtract))
                wtv(c4)
                e.engine_nop().then_inc(s_v2, 1)
                e.wait_ge(s_s1, gl + 1)
                c5 = mkv(e.reduce_sum(smx[:, :, 1:2], e3, axis=AX.X))
                wtv(c5)
                c6 = mkv(e.reciprocal(smx[:, 0:4, 1:2], smx[:, 0:4, 1:2]))
                wtv(c6)
                b1_, b2_ = bass_m.broadcast_tensor_aps(e3, smx[:, :, 1:2])
                c7 = mkv(e.tensor_tensor(wgt[:].rearrange("p (h s) -> p h s", h=4),
                                         b1_, b2_, ALU.mult))
                wtv(c7)
                e.engine_nop().then_inc(s_v2, 1)
            e.wait_ge(s_t2, i * 4 + 4)
            cv = mkv(e.tensor_copy(wTt[:], pt[0:24, :]))
            wtv(cv)
            e.engine_nop().then_inc(s_v3, 1)

        def s6_s(e, i=i):
            mks = lambda x: ctx.mark("scalar", x)
            wts = lambda n: ctx.wt("scalar", e, n)
            for j in range(4):
                gl = i * 4 + j
                e.wait_ge(s_v2, 2 * gl + 1)
                ce = mks(e.activation(exw[:], exw[:], AF.Exp))
                wts(ce)
                e.nop().then_inc(s_s1, 1)
                e.wait_ge(s_v2, 2 * gl + 2)
                if gl >= 1:
                    e.wait_ge(s_t2, gl)
                cw = mks(e.activation(wgtb[:], wgt[:], AF.Copy))
                wts(cw)
                e.nop().then_inc(s_s2, 1)
        tile_steps.append(dict(tensor=s6_t, vector=s6_v, scalar=s6_s))

        rounds = [(h, jb) for h in range(NH0) for jb in range(BRN)]
        def s7_t(e, i=i):
            for n, (h, jb) in enumerate(rounds):
                bc = i * 24 + n
                if bc >= 2:
                    e.wait_ge(s_be, bc - 1)
                r = h * 6 + jb
                e.matmul(p67[:, bc % 2], selb[:, r * 128:(r + 1) * 128], wTt[:],
                         start=True, stop=True).then_inc(s_bc, 1)

        def s7_s(e, i=i):
            for n, (h, jb) in enumerate(rounds):
                bc = i * 24 + n
                e.wait_ge(s_bc, bc + 1)
                if bc >= 2:
                    m2 = bc - 2
                    n2 = m2 % 24
                    if rounds[n2][0] < 3:
                        e.wait_ge(s_cv, m2 + 1)
                    else:
                        e.wait_ge(s_cp, (m2 // 24) * 6 + max(0, n2 - 17))
                e.activation(wbc[:, bc % 2, :], p67[:, bc % 2], AF.Copy
                             ).then_inc(s_be, 1)

        def s7_v(e, i=i, par=par):
            mkv = lambda x: ctx.mark("vector", x)
            wtv = lambda n: ctx.wt("vector", e, n)
            last = {}
            for n, (h, jb) in enumerate(rounds):
                bc = i * 24 + n
                if h == 3:
                    e.engine_nop().then_inc(s_cv, 1)
                    continue
                e.wait_ge(s_be, bc + 1)
                for s2 in range(2):
                    sl = 2 * h + s2
                    if jb == 0:
                        last[sl] = mkv(e.tensor_tensor(facc[:, sl, :],
                                                       brt[:, jb, sl, :],
                                                       wbc[:, bc % 2, :], ALU.mult))
                    else:
                        ts = (jb % 2) * 2 + s2
                        tmk = mkv(e.tensor_tensor(tvv[:, ts, :],
                                                  brt[:, jb, sl, :],
                                                  wbc[:, bc % 2, :], ALU.mult))
                        wtv(tmk)
                        wtv(last[sl])
                        last[sl] = mkv(e.tensor_tensor(facc[:, sl, :],
                                                       facc[:, sl, :],
                                                       tvv[:, ts, :], ALU.add))
                e.engine_nop().then_inc(s_cv, 1)

        def s7_p(e, i=i, par=par):
            mkp = lambda x: ctx.mark("gpsimd", x)
            wtp = lambda n: ctx.wt("gpsimd", e, n)
            last = {}
            for n, (h, jb) in enumerate(rounds):
                bc = i * 24 + n
                if h != 3:
                    continue
                e.wait_ge(s_be, bc + 1)
                for s2 in range(2):
                    sl = 6 + s2
                    if jb == 0:
                        last[sl] = mkp(e.tensor_tensor(facc[:, sl, :],
                                                       brt[:, jb, sl, :],
                                                       wbc[:, bc % 2, :], ALU.mult))
                    else:
                        ts = (jb % 2) * 2 + s2
                        tmk = mkp(e.tensor_tensor(tpp[:, ts, :],
                                                  brt[:, jb, sl, :],
                                                  wbc[:, bc % 2, :], ALU.mult))
                        wtp(tmk)
                        wtp(last[sl])
                        last[sl] = mkp(e.tensor_tensor(facc[:, sl, :],
                                                       facc[:, sl, :],
                                                       tpp[:, ts, :], ALU.add))
                e.engine_nop().then_inc(s_cp, 1)
        tvv = sb["g_tv"]; tpp = sb["g_tp"]
        tile_steps.append(dict(tensor=s7_t, scalar=s7_s, vector=s7_v, gpsimd=s7_p))

        def s8_s(e, i=i):
            mks = lambda x: ctx.mark("scalar", x)
            wts = lambda n: ctx.wt("scalar", e, n)
            cf = mks(e.square(f2[:].rearrange("p a b -> p (a b)"),
                              facc[:].rearrange("p a b -> p (a b)")))
            wts(cf)
            e.nop().then_inc(s_r1, 1)
            for h in range(NH0):
                gr = i * 4 + h
                e.wait_ge(s_om, 2 * gr + 1)
                c1 = mks(e.activation(rst4[0:1, h, :], p45[0:1, h % 2, 0:512],
                                      AF.Ln, bias=eps5[0:1, 0:1], scale=1.0 / 256.0))
                wts(c1)
                c2 = mks(e.activation(rst4[0:1, h, :], rst4[0:1, h, :], AF.Exp,
                                      scale=-0.5))
                wts(c2)
                c3 = mks(e.activation(rstb4[0:1, h, :], rst4[0:1, h, :], AF.Copy))
                wts(c3)
                e.nop().then_inc(s_r2, 1)

        def s8_t(e, i=i):
            e.wait_ge(s_r1, i + 1)
            for h in range(NH0):
                gr = i * 4 + h
                if gr >= 2:
                    e.wait_ge(s_r2, gr - 1)
                for s2 in range(2):
                    mm = e.matmul(p45[0:1, h % 2, 0:512], onescb[:],
                                  f2[:, 2 * h + s2, :], start=(s2 == 0),
                                  stop=(s2 == 1))
                mm.then_inc(s_om, 1)
                e.wait_ge(s_r2, gr + 1)
                if gr >= 2:
                    e.wait_ge(s_rb, gr - 1)
                e.matmul(p67[:, h % 2], ones1b2[:], rstb4[0:1, h, :],
                         start=True, stop=True).then_inc(s_om, 1)

        def s8_v(e, i=i):
            for h in range(NH0):
                gr = i * 4 + h
                e.wait_ge(s_om, 2 * gr + 2)
                for s2 in range(2):
                    tt = e.scalar_tensor_tensor(fusedb[:, 2 * h + s2, :],
                                                facc[:, 2 * h + s2, :],
                                                onc[:, s2:s2 + 1],
                                                p67[:, h % 2], ALU.mult, ALU.mult)
                tt.then_inc(s_rb, 1)
        tile_steps.append(dict(scalar=s8_s, tensor=s8_t, vector=s8_v))

        def s9_t(e, i=i):
            for j in range(4):
                go = i * 4 + j
                if go >= 1:
                    e.wait_ge(s_y0v, 2 * go)
                for dc in range(2):
                    for fb in range(8):
                        mm = e.matmul(po[:, dc], fusedb[:, fb, j * 128:(j + 1) * 128],
                                      ows[:, fb, dc * 512:(dc + 1) * 512],
                                      start=(fb == 0), stop=(fb == 7))
                    mm.then_inc(s_lg, 1) if False else mm.then_inc(sem["g_op"], 1)

        def s9_v(e, i=i, par=par):
            for j in range(4):
                go = i * 4 + j
                e.wait_ge(sem["g_op"], 2 * go + 2)
                if go >= 1:
                    e.wait_ge(s_oe, 16 * go)
                e.wait_ge(s_ld, 16 * (NT5 * 7 + i * 4 + j + 1) + 16 * 7) if False else None
                e.wait_ge(sem["g_xl"], 16 * (i * 4 + j + 1))
                for dc in range(2):
                    e.tensor_tensor(y0sb[:, dc * 512:(dc + 1) * 512],
                                    po[:, dc], xt[:, dc * 512:(dc + 1) * 512],
                                    ALU.add).then_inc(s_y0v, 1)

        def s9_g(e, i=i, c0=c0):
            for j in range(4):
                go = i * 4 + j
                if go >= 1:
                    e.wait_ge(s_y0v, 2 * go)
                e.dma_start(out=xt[:], in_=xD[c0 + j * 128:c0 + (j + 1) * 128, :]
                            ).then_inc(sem["g_xl"], 16)

        def s9_sy(e, i=i, c0=c0):
            for j in range(4):
                go = i * 4 + j
                e.wait_ge(s_y0v, 2 * go + 2)
                e.dma_start(out=y0D[c0 + j * 128:c0 + (j + 1) * 128, :],
                            in_=y0sb[:]).then_inc(s_oe, 16)
            e.wait_ge(s_oe, 16 * 4 * (i + 1))
        tile_steps.append(dict(tensor=s9_t, vector=s9_v, sync=s9_sy, gpsimd=s9_g))
        steps.extend(tile_steps[:EG_NSTEP])

    return steps


FIRK_ORDER = [0, 1, 2, 3, 4, 5]


def specs_phase_eg():
    inputs = {
        "xn_fm": INTER["xn_fm"], "x": ([T, D], F32),
        "delta_fm": ([8, 128, T], BF16), "v_fm": INTER["v_fm"],
        "w1x": ([D, D], F32), "w1s": ([128, D], F32),
        "mlp_w2": ([D, 24], F32), "mlp_b2": ([24], F32),
        "o_w": ([D, D], F32), "onormc": ([128, 2], F32),
        "b1c": ([128, 8], F32), "selmat": ([24, 24 * 128], BF16),
        "ones1f": ([1, 128], F32), "gate_log_temp": ([4], F32),
        "c_eps5": ([1, 1], F32), "c_one": ([1, 1], F32),
        "ones_col": ([128, 1], BF16), "ones1b": ([128, 128], BF16),
        "eyeb": ([128, 128], BF16),
    }
    for k in FIRK:
        inputs["fir%d_fm" % k] = ([8, 128, T], BF16)
    outputs = {"y0": ([T, D], F32), "stats_raw": ([BRN, 3, 4, T], BF16)}
    sbuf = {
        "eyeb": ([128, 128], BF16),
        "g_brt": ([128, BRN, 8, 512], BF16),
        "g_xnt": ([128, 8, 512], BF16),
        "g_xt": ([128, 1024], F32),
        "g_x2": ([128, 8, 512], BF16), "g_ax": ([128, 8, 512], BF16),
        "g_stage": ([1, 12, 512], BF16),
        "g_sder": ([128, 512], BF16), "g_stmp": ([24, 512], F32),
        "g_stmp2": ([24, 512], F32),
        "g_hgate": ([128, 8, 512], BF16),
        "g_wgt": ([128, 24], F32), "g_wgtb": ([128, 24], BF16),
        "g_smx": ([128, 4, 2], F32), "g_exw": ([128, 24], F32),
        "g_wTt": ([24, 512], BF16), "g_wbc": ([128, 2, 512], BF16),
        "g_tv": ([128, 4, 512], BF16), "g_tp": ([128, 4, 512], BF16),
        "g_facc": ([128, 8, 512], BF16),
        "g_rst4": ([1, 4, 512], F32), "g_rstb4": ([1, 4, 512], BF16),
        "g_fusedb": ([128, 8, 512], BF16),
        "g_y0sb": ([128, 1024], F32),
        "g_w1x": ([128, 8, D], BF16), "g_w1s": ([128, D], BF16),
        "g_w2s": ([128, 8, 24], BF16), "g_ows": ([128, 8, D], BF16),
        "g_onc": ([128, 2], F32), "g_b1c": ([128, 8], F32),
        "g_selb": ([24, 24 * 128], BF16),
        "g_on1f": ([1, 128], F32), "g_b2r": ([1, 24], F32),
        "g_glt": ([1, 4], F32), "g_b2bc": ([128, 24], F32),
        "g_invtbc": ([128, 24], F32), "g_eps5": ([1, 1], F32),
        "g_onec": ([1, 1], F32), "g_onescb": ([128, 1], BF16),
        "g_ones1b2": ([1, 128], BF16),
    }
    psum = {
        "g_p45": ([128, 2, 512], F32), "g_p67": ([128, 2, 512], F32),
        "g_pt": ([128, 512], BF16), "g_po": ([128, 2, 512], F32),
    }
    sems = ["g_w", "g_ld", "g_sm", "g_se", "g_so", "g_si", "g_px", "g_pa",
            "g_d1", "g_d2", "g_d3", "g_hg", "g_he", "g_lg", "g_v1", "g_s1",
            "g_v2", "g_s2", "g_t2", "g_v3", "g_bc", "g_be", "g_cv", "g_cp",
            "g_r1", "g_r2", "g_rb", "g_om", "g_y0v", "g_oe", "g_i1", "g_i2",
            "g_i3", "g_op", "g_xl"]
    return inputs, outputs, sbuf, psum, sems


def host_prep_eg(z):
    out = {}
    w1 = np.asarray(z["L0_attn_mlp_w1"])
    out["w1x"] = np.ascontiguousarray(w1[0:1024])
    w1s = np.zeros((128, D), np.float32)
    for p in range(128):
        grp, r2 = p // 32, p % 32
        if r2 < 24:
            br, h = r2 // 4, r2 % 4
            w1s[p] = w1[1024 + h * 24 + br * 4 + grp]
    out["w1s"] = w1s
    out["mlp_w2"] = np.ascontiguousarray(z["L0_attn_mlp_w2"])
    out["mlp_b2"] = np.ascontiguousarray(z["L0_attn_mlp_b2"])
    out["o_w"] = np.ascontiguousarray(z["L0_attn_o_w"])
    onorm = np.asarray(z["L0_attn_onorm_w"])
    out["onormc"] = np.stack([onorm[0:128], onorm[128:256]], 1).astype(np.float32)
    out["b1c"] = np.asarray(z["L0_attn_mlp_b1"]).reshape(8, 128).T.copy()
    sel = np.zeros((24, 24, 128), np.float32)
    for r in range(24):
        sel[r, r, :] = 1.0
    out["selmat"] = sel.reshape(24, 24 * 128).astype(ml_dtypes.bfloat16)
    out["ones1f"] = np.ones((1, 128), np.float32)
    out["gate_log_temp"] = np.ascontiguousarray(z["L0_attn_gate_log_temp"])
    out["c_eps5"] = np.full((1, 1), 1e-5, np.float32)
    out["c_one"] = np.full((1, 1), 1.0, np.float32)
    return out


YOFF = 256

def phase_h(ctx, sb, ps, sem, nm):
    aps = ctx.aps
    xnD = aps[nm["xn_fm"]]
    yD = aps[nm["y"]]
    wgD, bgD = aps[nm["Wg"]], aps[nm["bg"]]
    hgD = aps[nm["hg"]]
    cmbD = aps[nm["cmb"]]
    tok0 = nm.get("tok0", 0)
    toks = nm.get("tokens", T)
    NTT = toks // 128
    NHLF = toks // 4096
    NT2 = NTT // 2
    NT2 = globals().get('H_NT2', NT2)
    NK = 32
    pfx = nm.get("pfx", "h")
    Bu = lambda n: sb[pfx + n]
    xnt2 = Bu("_xnt2"); wg_s = Bu("_wg"); bgbc = Bu("_bgbc"); g1 = Bu("_g1")
    bgr = Bu("_bgr")
    cmb_s = Bu("_cmb"); xnh = Bu("_xnh"); wac = Bu("_wac")
    t0 = Bu("_t0"); hgw = Bu("_hgw")
    w2s = Bu("_w2s"); hgt = Bu("_hgt"); cmbt = Bu("_cmbt")
    yw = Bu("_yw"); onesr = Bu("_onesr"); on1f = Bu("_on1f")
    pA = ps[pfx + "_pA"]; pB = ps[pfx + "_pB"]
    s = lambda n: sem[pfx + n]
    (s_ld0, s_bg, s_mm1, s_gv, s_gs, s_cs, s_xn, s_wc, s_k, s_sl, s_ml, s_hs,
     s_hcd, s_w2, s_hg2, s_p2, s_ev2, s_ac, s_ld1, s_wc1, s_hs1, s_ac1) = (
        s(n) for n in (
        "_ld0", "_bg", "_mm1", "_gv", "_gs", "_cs", "_xn", "_wc", "_k", "_sl",
        "_ml", "_hs", "_hcd", "_w2", "_hg2", "_p2", "_ev2", "_ac", "_ld1",
        "_wc1", "_hs1", "_ac1"))

    steps = []

    def g0_g(e):
        e.sem_inc(s_ml, 64)
        e.sem_inc(s_hs, 16 * 64)
        e.sem_inc(s_hs1, 16 * 64)
        e.sem_inc(s_ev2, 64)
        e.sem_inc(s_p2, 64)
        e.sem_inc(s_hcd, 64)
        e.sem_inc(s_ac, 16 * 64)
        e.sem_inc(s_ac1, 16 * 64)
        if "y_init" in nm:
            e.dma_start(out=yD[YOFF:YOFF + T, :], in_=aps[nm["y_init"]]
                        ).then_inc(s_ld0, 16)
        e.dma_start(out=wg_s[:], in_=wgD.rearrange("(db p) x -> p db x", p=128)
                    ).then_inc(s_ld0, 16)
        e.dma_start(out=onesr[:], in_=aps["onesr512"]).then_inc(s_ld0, 16)
        e.dma_start(out=on1f[:], in_=aps["ones1f"]).then_inc(s_ld0, 16)
        e.dma_start(out=bgr[:], in_=bgD[None, :]).then_inc(s_ld0, 16)
        e.wait_ge(s_ld0, 64 + (16 if "y_init" in nm else 0))

    def g0_t(e):
        e.wait_ge(s_ld0, 64 + (16 if "y_init" in nm else 0))
        e.matmul(pA[:, 0, 0:8], on1f[:], bgr[:], start=True, stop=True
                 ).then_inc(s_bg, 1)

    def g0_v(e):
        e.wait_ge(s_bg, 1)
        cb = ctx.mark("vector", e.tensor_copy(bgbc[:], pA[:, 0, 0:8]))
        ctx.wt("vector", e, cb)
    steps.append(dict(gpsimd=g0_g, tensor=g0_t, vector=g0_v))

    def g1_g(e):
        for i in range(NTT):
            if i >= 2:
                e.wait_ge(s_mm1, i - 1)
            e.dma_start(out=xnt2[:, i % 2],
                        in_=xnD[:, :, tok0 + i * 128:tok0 + (i + 1) * 128]
                        .rearrange("d p t -> p d t")
                        ).then_inc(s_ld1 if i % 2 else s_ld0, 16)

    ng0 = 5 if "y_init" in nm else 4

    def g1_t(e):
        for i in range(NTT):
            if i % 2 == 0:
                e.wait_ge(s_ld0, 16 * (ng0 + i // 2 + 1))
            else:
                e.wait_ge(s_ld1, 16 * (i // 2 + 1))
            if i >= 2:
                e.wait_ge(s_gv, 3 * (i - 1))
            for db in range(8):
                mm = e.matmul(pA[:, i % 2, 0:8], xnt2[:, i % 2, db, :],
                              wg_s[:, db, :], start=(db == 0), stop=(db == 7))
            mm.then_inc(s_mm1, 1)

    def g1_v(e):
        mkv = lambda x: ctx.mark("vector", x)
        wtv = lambda n: ctx.wt("vector", e, n)
        for i in range(NTT):
            e.wait_ge(s_mm1, i + 1)
            par = i % 2
            c1 = mkv(e.tensor_tensor(g1[:, par, 0:8], pA[:, par, 0:8], bgbc[:],
                                     ALU.add))
            wtv(c1)
            e.engine_nop().then_inc(s_gv, 1)
            c2 = mkv(e.reduce_max(g1[:, par, 8:9], g1[:, par, 0:8], axis=AX.X))
            wtv(c2)
            c3 = mkv(e.tensor_scalar(g1[:, par, 16:24], g1[:, par, 0:8], g1[:, par, 8:9], None,
                                     ALU.is_equal))
            wtv(c3)
            c4 = mkv(e.scalar_tensor_tensor(g1[:, par, 24:32], g1[:, par, 16:24], -1e30,
                                            g1[:, par, 0:8], ALU.mult, ALU.add))
            wtv(c4)
            c5 = mkv(e.reduce_max(g1[:, par, 9:10], g1[:, par, 24:32], axis=AX.X))
            wtv(c5)
            c6 = mkv(e.tensor_tensor(g1[:, par, 10:11], g1[:, par, 8:9], g1[:, par, 9:10],
                                     ALU.subtract))
            wtv(c6)
            e.engine_nop().then_inc(s_gv, 1)
            e.wait_ge(s_gs, i + 1)
            c7 = mkv(e.tensor_scalar(g1[:, par, 12:13], g1[:, par, 11:12], -1.0, 1.0,
                                     ALU.mult, ALU.add))
            c8 = mkv(e.tensor_scalar(g1[:, par, 32:40], g1[:, par, 24:32], g1[:, par, 9:10],
                                     None, ALU.is_equal))
            c9 = mkv(e.tensor_scalar(g1[:, par, 40:48], g1[:, par, 16:24], g1[:, par, 11:12],
                                     None, ALU.mult))
            wtv(c7); wtv(c8); wtv(c9)
            if i >= 2:
                e.wait_ge(s_cs, 16 * 8 * (i - 1))
            ca = mkv(e.scalar_tensor_tensor(cmb_s[:, par, :], g1[:, par, 32:40],
                                            g1[:, par, 12:13], g1[:, par, 40:48],
                                            ALU.mult, ALU.add))
            wtv(ca)
            e.engine_nop().then_inc(s_gv, 1)

    def g1_s(e):
        mks = lambda x: ctx.mark("scalar", x)
        wts = lambda n: ctx.wt("scalar", e, n)
        for i in range(NTT):
            par = i % 2
            e.wait_ge(s_gv, 3 * i + 2)
            cs = mks(e.activation(g1[:, par, 11:12], g1[:, par, 10:11], AF.Sigmoid))
            wts(cs)
            e.nop().then_inc(s_gs, 1)

    def g1_sy(e):
        for i in range(NTT):
            e.wait_ge(s_gv, 3 * i + 3)
            for e8 in range(8):
                e.dma_start(out=cmbD[e8 * NTT + i, :, None],
                            in_=cmb_s[:, i % 2, e8:e8 + 1]).then_inc(s_cs, 16)
        e.wait_ge(s_cs, 16 * 8 * NTT)
    steps.append(dict(gpsimd=g1_g, tensor=g1_t, vector=g1_v, scalar=g1_s,
                      sync=g1_sy))

    def p1_sy(e):
        xnv = xnD[:, :, tok0:tok0 + toks].rearrange(
            "d p (hf q) -> hf p d q", hf=NHLF)
        for half in range(NHLF):
            e.wait_ge(s_hcd, half * 64 + 64)
            e.dma_start(out=xnh[:], in_=xnv[half]).then_inc(s_xn, 16)

    def p1_g(e):
        wpkv = aps[nm["wpk"]].rearrange("e hc p m x -> (e hc) p m x")
        hgv = hgD.rearrange("(f2 j) p hb q -> f2 j p hb q", j=32)
        for half in range(NHLF):
            for ex in range(8):
                for hcp in range(4):
                    f2_ = (half * 8 + ex) * 4 + hcp
                    for u in range(2):
                        e.wait_ge(s_hcd, f2_ * 2 + u + 63)
                        e.dma_start(out=wac[:, u],
                                    in_=wpkv[ex * 8 + hcp * 2 + u]
                                    ).then_inc(s_wc1 if u else s_wc, 16)
                    for u in range(2):
                        base_u = f2_ * (2 * NK) + u * NK
                        hbase = (hcp * 2 + u) * 4
                        for hsub in range(4):
                            e.wait_ge(s_ml, base_u + hsub * 8 + 72)
                            e.dma_start(
                                out=hgv[ex * NHLF + half, :, :, hbase + hsub,
                                        :].rearrange("j p q -> p j q"),
                                in_=hgw[:, hsub % 2]
                                ).then_inc(s_hs1 if hsub % 2 else s_hs, 16)
        e.wait_ge(s_hs, 16 * (64 + NHLF * 128))
        e.wait_ge(s_hs1, 16 * (64 + NHLF * 128))

    def p1_t(e):
        with e.Fori(0, NHLF, 1) as half:
            e.wait_ge(s_xn, half * 16 + 16)
            with e.Fori(0, 8, 1) as ex:
                with e.Fori(0, 4, 1) as hcp:
                    f2_ = (half * 8 + ex) * 4 + hcp
                    for u in range(2):
                        fhc = f2_ * 2 + u
                        base_u = f2_ * (2 * NK) + u * NK
                        e.wait_ge(s_wc1 if u else s_wc, f2_ * 16 + 16)
                        for k in range(NK):
                            hsub, tt = k // 8, k % 8
                            ctx.rwait(e, s_ml, base_u, k + 63)
                            for db in range(8):
                                e.matmul(pA[:, k % 2],
                                         wac[:, u, db, hsub * 128:(hsub + 1) * 128],
                                         xnh[:, db, tt * 512:(tt + 1) * 512],
                                         start=(db == 0), stop=False)
                            e.matmul(pA[:, k % 2],
                                     wac[0:1, u, 16, hsub * 128:(hsub + 1) * 128],
                                     onesr[0:1, 0:512], start=False, stop=True
                                     ).then_inc(s_k, 1)
                            for db in range(8):
                                e.matmul(pB[:, k % 2],
                                         wac[:, u, 8 + db, hsub * 128:(hsub + 1) * 128],
                                         xnh[:, db, tt * 512:(tt + 1) * 512],
                                         start=(db == 0), stop=False)
                            e.matmul(pB[:, k % 2],
                                     wac[0:1, u, 17, hsub * 128:(hsub + 1) * 128],
                                     onesr[0:1, 0:512], start=False, stop=True
                                     ).then_inc(s_k, 1)
                        e.nop().then_inc(s_hcd, 1)

    def p1_s(e):
        with e.Fori(0, NHLF, 1) as half:
            with e.Fori(0, 8, 1) as ex:
                with e.Fori(0, 4, 1) as hcp:
                    f2_ = (half * 8 + ex) * 4 + hcp
                    for u in range(2):
                        base_u = f2_ * (2 * NK) + u * NK
                        base2_u = f2_ * (4 * NK) + u * 2 * NK
                        for k in range(NK):
                            ctx.rwait(e, s_k, base2_u, 2 * k + 2)
                            ctx.rwait(e, s_ml, base_u, k + 63)
                            e.activation(t0[:, k % 2], pB[:, k % 2], AF.Silu
                                         ).then_inc(s_sl, 1)

    def p1_v(e):
        with e.Fori(0, NHLF, 1) as half:
            with e.Fori(0, 8, 1) as ex:
                with e.Fori(0, 4, 1) as hcp:
                    f2_ = (half * 8 + ex) * 4 + hcp
                    for u in range(2):
                        base_u = f2_ * (2 * NK) + u * NK
                        base_st2 = (f2_ * 4 + u * 2) * 16
                        for k in range(NK):
                            hsub, tt = k // 8, k % 8
                            ctx.rwait(e, s_sl, base_u, k + 1)
                            if tt == 0:
                                ctx.rwait(e, s_hs1 if hsub % 2 else s_hs,
                                          base_st2, (hsub // 2 + 64) * 16)
                            e.tensor_tensor(hgw[:, hsub % 2, tt * 512:(tt + 1) * 512],
                                            pA[:, k % 2], t0[:, k % 2], ALU.mult
                                            ).then_inc(s_ml, 1)
    steps.append(dict(sync=p1_sy, gpsimd=p1_g, tensor=p1_t, scalar=p1_s,
                      vector=p1_v))

    def p2_g(e):
        yv = yD.rearrange("(r p) d -> r p d", p=128)
        for ex in range(H_NE2):
            e.wait_ge(s_p2, ex * NTT + 64)
            if H_P2LD & 1:
                e.dma_start(out=w2s[:], in_=aps[nm["w2pk"]][ex]).then_inc(s_w2, 32)
            else:
                e.nop().then_inc(s_w2, 32)
            for ttp in range(NT2):
                for u in range(2):
                    j = 2 * ttp + u
                    ft_u = ex * NTT + ttp * 2
                    e.wait_ge(s_ev2, ft_u + u + 64)
                    if H_P2LD & 4:
                        e.dma_start(out=yv[j - 2 + (tok0 + YOFF) // 128],
                                    in_=yw[:, u],
                                    **(dict(accum_op=ALU.add) if H_ACC else {})
                                    ).then_inc(s_ac1 if u else s_ac, 16)
                    else:
                        e.nop().then_inc(s_ac1 if u else s_ac, 16)
                    e.wait_ge(s_p2, ft_u + u + 64)
                    if H_P2LD & 2:
                        e.dma_start(out=hgt[:], in_=hgD[ft_u + u]
                                    ).then_inc(s_hg2, 16)
                        e.dma_start(out=cmbt[:], in_=cmbD[ft_u + u, :, None]
                                    ).then_inc(s_hg2, 16)
                    else:
                        e.nop().then_inc(s_hg2, 32)
            for u in range(2):
                j = NTT - 2 + u
                e.wait_ge(s_ev2, ex * NTT + j + 65)
                if H_P2LD & 4:
                    e.dma_start(out=yD[tok0 + j * 128 + YOFF:
                                      tok0 + (j + 1) * 128 + YOFF, :],
                                in_=yw[:, u],
                                **(dict(accum_op=ALU.add) if H_ACC else {})
                                ).then_inc(s_ac1 if u else s_ac, 16)
                else:
                    e.nop().then_inc(s_ac1 if u else s_ac, 16)
        e.wait_ge(s_ac, 16 * (64 + H_NE2 * (NTT + 2) // 2))
        e.wait_ge(s_ac1, 16 * (64 + H_NE2 * (NTT + 2) // 2))

    def p2_t(e):
        with e.Fori(0, H_NE2, 1) as ex:
            e.wait_ge(s_w2, ex * 32 + 32)
            with e.Fori(0, NT2, 1) as ttp:
                for u in range(2):
                    j = 2 * ttp + u
                    ft_u = ex * NTT + ttp * 2
                    ft32 = (ex * NTT + ttp * 2) * 32
                    ctx.rwait(e, s_hg2, ft32, u * 32 + 32)
                    ctx.rwait(e, s_ev2, ft_u, u + 63)
                    if H_P2MM:
                        for dc in range(2):
                            pX = pA if dc == 0 else pB
                            for hb in range(32):
                                e.matmul(pX[:, u, :], hgt[:, hb, :],
                                         w2s[:, hb, dc * 512:(dc + 1) * 512],
                                         start=(hb == 0), stop=False)
                            mm = e.matmul(pX[:, u, :], onesr[0:1, 0:128],
                                          w2s[0:1, 32, dc * 512:(dc + 1) * 512],
                                          start=False, stop=True)
                        mm.then_inc(s_p2, 1)
                    else:
                        e.nop().then_inc(s_p2, 1)

    def p2_v(e):
        with e.Fori(0, H_NE2, 1) as ex:
            with e.Fori(0, NT2, 1) as ttp:
                for u in range(2):
                    j = 2 * ttp + u
                    ft_u = ex * NTT + ttp * 2
                    ac16h = (ex * ((NTT + 2) // 2) + ttp) * 16
                    ctx.rwait(e, s_p2, ft_u, u + 65)
                    ctx.rwait(e, s_ac1 if u else s_ac, ac16h, 65 * 16)
                    if H_P2EV:
                        e.tensor_scalar(yw[:, u, 0:512], pA[:, u, :],
                                        cmbt[:, 0:1], None, ALU.mult)
                        e.tensor_scalar(yw[:, u, 512:1024], pB[:, u, :],
                                        cmbt[:, 0:1], None, ALU.mult
                                        ).then_inc(s_ev2, 1)
                    else:
                        e.engine_nop().then_inc(s_ev2, 1)
    steps.append(dict(gpsimd=p2_g, tensor=p2_t, vector=p2_v))
    return steps[:H_NSTEP]


def specs_phase_h(nm):
    toks = nm.get("tokens", T)
    NTT = toks // 128
    pfx = nm.get("pfx", "h")
    inputs = {
        nm["xn_fm"]: INTER["xn_fm"],
        nm["wpk"]: ([E, 8, 128, 18, 512], F32),
        nm["w2pk"]: ([E, 128, 33, D], F32),
        nm["Wg"]: ([D, E], F32), nm["bg"]: ([E], F32),
        "onesr512": ([1, 512], BF16), "ones1f": ([1, 128], F32),
    }
    outputs = {
        nm["y"]: ([YOFF + T, D], F32),
        nm["hg"]: ([E * NTT, 128, 32, 128], BF16),
        nm["cmb"]: ([E * NTT, 128], F32),
    }
    sbuf = {
        pfx + "_xnt2": ([128, 2, 8, 128], BF16),
        pfx + "_wg": ([128, 8, E], BF16),
        pfx + "_bgbc": ([128, 8], F32),
        pfx + "_bgr": ([1, 8], F32),
        pfx + "_g1": ([128, 2, 48], F32),
        pfx + "_cmb": ([128, 2, 8], F32),
        pfx + "_xnh": ([128, 8, 4096], BF16),
        pfx + "_wac": ([128, 2, 18, 512], BF16),
        pfx + "_t0": ([128, 2, 512], BF16),
        pfx + "_hgw": ([128, 2, 4096], BF16),
        pfx + "_w2s": ([128, 33, D], BF16),
        pfx + "_hgt": ([128, 32, 128], BF16),
        pfx + "_cmbt": ([128, 1], F32),
        pfx + "_yw": ([128, 2, D], F32),
        pfx + "_onesr": ([1, 512], BF16),
        pfx + "_on1f": ([1, 128], F32),
    }
    psum = {pfx + "_pA": ([128, 2, 512], F32), pfx + "_pB": ([128, 2, 512], F32)}
    sems = [pfx + n for n in ("_ld0", "_bg", "_mm1", "_gv", "_gs", "_cs",
                              "_xn", "_wc", "_k", "_sl", "_ml", "_hs", "_hcd",
                              "_w2", "_hg2", "_p2", "_ev2", "_ac", "_ld1",
                              "_wc1", "_hs1", "_ac1")]
    return inputs, outputs, sbuf, psum, sems


L0FFN_NM = dict(xn_fm="xn_fm", y="y0p", wpk="ffn_wpk", w2pk="ffn_w2pk",
                Wg="ffn_Wg", bg="ffn_bg", hg="hg_scr", cmb="cmb_scr", pfx="h")


def host_pack_ffn(WA, WB, bA, bB, W2, b2):
    Ee = WA.shape[0]
    wpk = np.zeros((Ee, 8, 128, 18, 512), np.float32)
    wa = np.asarray(WA, np.float32).reshape(Ee, 8, 128, 8, 512)
    wb = np.asarray(WB, np.float32).reshape(Ee, 8, 128, 8, 512)
    for db in range(8):
        wpk[:, :, :, db, :] = wa[:, db].transpose(0, 2, 1, 3)
        wpk[:, :, :, 8 + db, :] = wb[:, db].transpose(0, 2, 1, 3)
    ba = np.asarray(bA, np.float32).reshape(Ee, 8, 512)
    bb = np.asarray(bB, np.float32).reshape(Ee, 8, 512)
    wpk[:, :, 0, 16, :] = ba
    wpk[:, :, 0, 17, :] = bb
    w2pk = np.zeros((Ee, 128, 33, W2.shape[2]), np.float32)
    w2r = np.asarray(W2, np.float32).reshape(Ee, 32, 128, -1)
    w2pk[:, :, 0:32, :] = w2r.transpose(0, 2, 1, 3)
    w2pk[:, 0, 32, :] = np.asarray(b2, np.float32)
    return wpk, w2pk


def phase_j(ctx, sb, ps, sem):
    aps = ctx.aps
    xnD = aps["xn_fm"]
    outD = aps["xnr_fm"]
    xnt = sb["j_xnt"]; sq2 = sb["j_sq2"]; lnw = sb["j_lnw"]
    rin = sb["j_rin"]; rinb = sb["j_rinb"]; outb = sb["j_out"]
    ones1 = sb["j_ones1"]; onesc = sb["j_onesc"]; epsb = sb["j_eps"]
    pn = ps["j_pn"]; pb = ps["j_pb"]
    s_w, s_xt, s_sq, s_nm, s_ri, s_bc, s_ev, s_st = (
        sem[k] for k in ("j_w", "j_xt", "j_sq", "j_nm", "j_ri", "j_bc",
                         "j_ev", "j_st"))

    def eng_gpsimd(e):
        e.dma_start(out=lnw[:], in_=aps["ln_w"].rearrange("(db p) -> p db", p=128),
                    allow_slow_non_contiguous=True).then_inc(s_w, 16)
        e.dma_start(out=ones1[:], in_=aps["ones1b"][0:1, :]).then_inc(s_w, 16)
        e.dma_start(out=onesc[:], in_=aps["ones_col"]).then_inc(s_w, 16)
        e.dma_start(out=epsb[:], in_=aps["c_epsf"]).then_inc(s_w, 16)

    def eng_sync(e):
        for i in range(NT5):
            c0 = i * 512
            par = i % 2
            if i >= 2:
                e.wait_ge(s_ev, 8 * (i - 1))
                e.dma_start(out=outD[:, :, c0 - 1024:c0 - 512].rearrange(
                    "d p t -> p d t"), in_=outb[:, par]).then_inc(s_st, 16)
            e.dma_start(out=xnt[:, par],
                        in_=xnD[:, :, c0:c0 + 512].rearrange("d p t -> p d t")
                        ).then_inc(s_xt, 16)
        for i in (NT5 - 2, NT5 - 1):
            e.wait_ge(s_ev, 8 * (i + 1))
            e.dma_start(out=outD[:, :, i * 512:(i + 1) * 512].rearrange(
                "d p t -> p d t"), in_=outb[:, i % 2]).then_inc(s_st, 16)
        e.wait_ge(s_st, 16 * NT5)

    def eng_scalar(e):
        mks = lambda x: ctx.mark("scalar", x)
        wts = lambda n: ctx.wt("scalar", e, n)
        e.wait_ge(s_w, 64)
        for i in range(NT5):
            par = i % 2
            e.wait_ge(s_xt, 16 * (i + 1))
            if i >= 2:
                e.wait_ge(s_nm, i - 1)
            for blk in range(8):
                e.activation(sq2[:, par, blk], xnt[:, par, blk], AF.Square
                             ).then_inc(s_sq, 1)
            e.wait_ge(s_nm, i + 1)
            c1 = mks(e.activation(rin[0:1, par], pn[0:1, par], AF.Ln,
                                  bias=epsb[:], scale=1.0 / D))
            wts(c1)
            c2 = mks(e.activation(rinb[0:1, par], rin[0:1, par], AF.Exp,
                                  scale=-0.5))
            wts(c2)
            e.nop().then_inc(s_ri, 1)

    def eng_tensor(e):
        e.wait_ge(s_w, 64)
        for i in range(NT5):
            par = i % 2
            if i >= 2:
                e.wait_ge(s_ri, i - 1)
            for blk in range(8):
                e.wait_ge(s_sq, 8 * i + blk + 1)
                mm = e.matmul(pn[:, par], onesc[:], sq2[:, par, blk],
                              start=(blk == 0), stop=(blk == 7))
            mm.then_inc(s_nm, 1)
            e.wait_ge(s_ri, i + 1)
            if i >= 2:
                e.wait_ge(s_ev, 8 * (i - 1))
            e.matmul(pb[:, par], ones1[:], rinb[0:1, par], start=True,
                     stop=True).then_inc(s_bc, 1)

    def eng_vector(e):
        for i in range(NT5):
            par = i % 2
            e.wait_ge(s_bc, i + 1)
            if i >= 2:
                e.wait_ge(s_st, 16 * (i - 1))
            for blk in range(8):
                e.scalar_tensor_tensor(outb[:, par, blk], xnt[:, par, blk],
                                       lnw[:, blk:blk + 1], pb[:, par],
                                       ALU.mult, ALU.mult).then_inc(s_ev, 1)

    return dict(gpsimd=eng_gpsimd, sync=eng_sync, scalar=eng_scalar,
                tensor=eng_tensor, vector=eng_vector)


def specs_phase_j():
    inputs = {
        "xn_fm": INTER["xn_fm"],
        "ln_w": ([D], F32),
        "ones1b": ([128, 128], BF16), "ones_col": ([128, 1], BF16),
        "c_epsf": ([1, 1], F32),
    }
    outputs = {"xnr_fm": ([8, 128, T], BF16)}
    sbuf = {
        "j_xnt": ([128, 2, 8, 512], BF16),
        "j_sq2": ([128, 2, 8, 512], BF16),
        "j_lnw": ([128, 8], F32),
        "j_rin": ([1, 2, 512], F32),
        "j_rinb": ([1, 2, 512], BF16),
        "j_out": ([128, 2, 8, 512], BF16),
        "j_ones1": ([1, 128], BF16), "j_onesc": ([128, 1], BF16),
        "j_eps": ([1, 1], F32),
    }
    psum = {"j_pn": ([1, 2, 512], F32), "j_pb": ([128, 2, 512], F32)}
    sems = ["j_w", "j_xt", "j_sq", "j_nm", "j_ri", "j_bc", "j_ev", "j_st"]
    return inputs, outputs, sbuf, psum, sems


def phase_i(ctx, sb, ps, sem):
    aps = ctx.aps
    xnD = aps["xn_fm"]
    wsb = {P: sb["i_w" + P] for P in "qkv"}
    xnt = sb["i_xnt"]; nmn = sb["i_nmn"]; pos = sb["i_pos"]; eex = sb["i_eex"]
    ofm = {"q": sb["i_oq"], "k": sb["i_ok"], "v": sb["i_ov"]}
    ttm = sb["i_ttm"]; sqs = sb["i_sqs"]; tmp = sb["i_tmp"]
    pp = ps["i_pp"]
    (s_w, s_xt, s_mm, s_ev, s_vn, s_ex, s_sl, s_vv, s_tp, s_ts, s_st,
     s_sqa) = (sem[k] for k in ("i_w", "i_xt", "i_mm", "i_ev", "i_vn", "i_ex",
                                "i_sl", "i_vv", "i_tp", "i_ts", "i_st",
                                "i_sqa"))
    PROJ = ("q", "k", "v")

    def eng_gpsimd(e):
        for P, nm in (("q", "wq"), ("k", "wk"), ("v", "wv")):
            e.dma_start(out=wsb[P][:],
                        in_=aps[nm].rearrange("(db p) f -> p db f", p=128)
                        ).then_inc(s_w, 16)
        for i in range(NT5):
            c0 = i * 512
            e.wait_ge(s_sl, 16 * i + 8)
            e.dma_start(out=aps["sq_fm"][:, :, c0:c0 + 512].rearrange(
                "d p t -> p d t"), in_=ofm["q"][:]).then_inc(s_st, 16)
            e.wait_ge(s_sl, 16 * (i + 1))
            e.dma_start(out=aps["sk_fm"][:, :, c0:c0 + 512].rearrange(
                "d p t -> p d t"), in_=ofm["k"][:]).then_inc(s_st, 16)
        e.wait_ge(s_sqa, 1)
        e.dma_start(out=aps["sqs"], in_=sqs[:]).then_inc(s_st, 16)
        e.wait_ge(s_st, 16 * (2 * NT5 + 1))

    def eng_sync(e):
        for i in range(NT5):
            c0 = i * 512
            par = i % 2
            if i >= 2:
                e.wait_ge(s_mm, 24 * (i - 1))
            e.dma_start(out=xnt[:, par],
                        in_=xnD[:, :, c0:c0 + 512].rearrange("d p t -> p d t")
                        ).then_inc(s_xt, 16)
            for j in range(4):
                jg = 4 * i + j
                e.wait_ge(s_tp, 16 * (64 * i + (j + 1) * 16))
                e.dma_start(out=aps["skt_tm"][jg],
                            in_=ttm[:, j % 2, 0].rearrange("p f q -> p (f q)")
                            ).then_inc(s_ts, 16)
                e.dma_start(out=aps["v_tm"][jg],
                            in_=ttm[:, j % 2, 1].rearrange("p f q -> p (f q)")
                            ).then_inc(s_ts, 16)
        e.wait_ge(s_ts, 16 * 2 * 4 * NT5)

    def eng_tensor(e):
        e.wait_ge(s_w, 48)
        for i in range(NT5):
            par5 = i % 2
            e.wait_ge(s_xt, 16 * (i + 1))
            for pi, P in enumerate(PROJ):
                for fb in range(8):
                    g = i * 24 + pi * 8 + fb
                    if g >= 2:
                        e.wait_ge(s_ev, g - 1)
                    for db in range(8):
                        mm = e.matmul(pp[:, g % 2],
                                      wsb[P][:, db, fb * 128:(fb + 1) * 128],
                                      xnt[:, par5, db, :],
                                      start=(db == 0), stop=(db == 7))
                    mm.then_inc(s_mm, 1)

    def eng_vector(e):
        mkv = lambda x: ctx.mark("vector", x)
        wtv = lambda n: ctx.wt("vector", e, n)
        cm = mkv(e.memset(sqs[:], 0.0))
        wtv(cm)
        for i in range(NT5):
            if i >= 1:
                e.wait_ge(s_st, 16 * 2 * i)
                e.wait_ge(s_tp, 16 * 64 * i)
            for pi, P in enumerate(("q", "k")):
                for fb in range(8):
                    g = i * 24 + pi * 8 + fb
                    gq = i * 16 + pi * 8 + fb
                    par = g % 2
                    e.wait_ge(s_mm, g + 1)
                    e.tensor_scalar(nmn[:, par], pp[:, par], 0.0,
                                    None, ALU.min).then_inc(s_vn, 1)
                    c2 = mkv(e.tensor_scalar(pos[:, par], pp[:, par], 0.0,
                                             None, ALU.max))
                    wtv(c2)
                    e.engine_nop().then_inc(s_ev, 1)
                    e.wait_ge(s_ex, gq + 1)
                    c3 = mkv(e.tensor_tensor(ofm[P][:, fb, :], eex[:, par],
                                             pos[:, par], ALU.add))
                    wtv(c3)
                    e.engine_nop().then_inc(s_sl, 1)
                    if P == "q":
                        c4 = mkv(e.reduce_sum(tmp[:], ofm["q"][:, fb, :],
                                              axis=AX.X))
                        wtv(c4)
                        c5 = mkv(e.tensor_tensor(sqs[:, fb:fb + 1],
                                                 sqs[:, fb:fb + 1], tmp[:],
                                                 ALU.add))
                        wtv(c5)
        e.engine_nop().then_inc(s_sqa, 1)

    def eng_scalar(e):
        mks = lambda x: ctx.mark("scalar", x)
        wts = lambda n: ctx.wt("scalar", e, n)
        for i in range(NT5):
            for pi in range(2):
                for fb in range(8):
                    gq = i * 16 + pi * 8 + fb
                    par = (i * 24 + pi * 8 + fb) % 2
                    e.wait_ge(s_vn, gq + 1)
                    c = mks(e.activation(eex[:, par], nmn[:, par], AF.Exp))
                    wts(c)
                    e.nop().then_inc(s_ex, 1)
            if i >= 1:
                e.wait_ge(s_tp, 16 * 64 * i)
            for fb in range(8):
                g = i * 24 + 16 + fb
                e.wait_ge(s_mm, g + 1)
                c = mks(e.activation(ofm["v"][:, fb, :], pp[:, g % 2], AF.Copy))
                wts(c)
                e.nop().then_inc(s_ev, 1)
            e.wait_ge(s_sl, 16 * (i + 1))
            for j in range(4):
                jg = 4 * i + j
                if jg >= 2:
                    e.wait_ge(s_ts, 16 * 2 * (jg - 1))
                for ki, P in enumerate(("k", "v")):
                    for fb in range(8):
                        d = e.dma_start_transpose(
                            ttm[:, j % 2, ki, fb],
                            ofm[P][:, fb, j * 128:(j + 1) * 128])
                        d.then_inc(s_tp, 16)

    return dict(gpsimd=eng_gpsimd, sync=eng_sync, tensor=eng_tensor,
                vector=eng_vector, scalar=eng_scalar)


def specs_phase_i():
    inputs = {
        "xn_fm": INTER["xn_fm"],
        "wq": ([D, D], F32), "wk": ([D, D], F32), "wv": ([D, D], F32),
    }
    outputs = {
        "sq_fm": ([8, 128, T], BF16), "sk_fm": ([8, 128, T], BF16),
        "skt_tm": ([NT1, 128, D], BF16), "v_tm": ([NT1, 128, D], BF16),
        "sqs": ([128, 8], F32),
    }
    sbuf = {
        "i_wq": ([128, 8, D], BF16), "i_wk": ([128, 8, D], BF16),
        "i_wv": ([128, 8, D], BF16),
        "i_xnt": ([128, 2, 8, 512], BF16),
        "i_nmn": ([128, 2, 512], F32), "i_pos": ([128, 2, 512], F32),
        "i_eex": ([128, 2, 512], F32),
        "i_oq": ([128, 8, 512], BF16), "i_ok": ([128, 8, 512], BF16),
        "i_ov": ([128, 8, 512], BF16),
        "i_ttm": ([128, 2, 2, 8, 128], BF16),
        "i_sqs": ([128, 8], F32), "i_tmp": ([128, 1], F32),
    }
    psum = {"i_pp": ([128, 2, 512], F32)}
    sems = ["i_w", "i_xt", "i_mm", "i_ev", "i_vn", "i_ex", "i_sl", "i_vv",
            "i_tp", "i_ts", "i_st", "i_sqa"]
    return inputs, outputs, sbuf, psum, sems


def phase_k(ctx, sb, ps, sem):
    aps = ctx.aps
    sqD, skD = aps["sq_fm"], aps["sk_fm"]
    sktD, vtD = aps["skt_tm"], aps["v_tm"]
    outD = aps["mixed_fm"]
    sqc = sb["k_sqc"]; skc = sb["k_skc"]; sktc = sb["k_sktc"]; vtc = sb["k_vtc"]
    vext = sb["k_vext"]; atm = sb["k_atm"]; Ssb = sb["k_Ssb"]; Sbf = sb["k_Sbf"]
    mnsb = sb["k_mnsb"]; memsb = sb["k_memsb"]; wbc = sb["k_wbc"]
    gbc = sb["k_gbc"]; sqssb = sb["k_sqssb"]; sqsbf = sb["k_sqsbf"]
    gater = sb["k_gater"]; gsig = sb["k_gsig"]; gsbf = sb["k_gsbf"]
    wrow = sb["k_wrow"]; wrow0 = sb["k_wrow0"]
    wexp = sb["k_wexp"]; wfin = sb["k_wfin"]
    wtmp = sb["k_wtmp"]; wbf = sb["k_wbf"]
    relc = sb["k_relc"]; wrel = sb["k_wrel"]; dinv = sb["k_dinv"]
    dcl = sb["k_dcl"]; ltm = sb["k_ltm"]; macc = sb["k_macc"]; dif = sb["k_dif"]
    mixs = sb["k_mixs"]; kttm = sb["k_kttm"]; minc = sb["k_minc"]
    ones1 = sb["k_ones1"]; onesc = sb["k_onesc"]
    pAT = ps["k_pAT"]; pLOC = ps["k_pLOC"]; pMEM = ps["k_pMEM"]
    pSUP = ps["k_pSUP"]; pREL = ps["k_pREL"]
    (s_w, s_pr, s_ld, s_tc, s_vc, s_t1, s_vm, s_t2, s_t3, s_t4, s_t5, s_rl,
     s_sv, s_sc, s_ma, s_mx, s_dv, s_vx, s_tp, s_ts, s_r1, s_r2) = (
        sem[k] for k in ("k_w", "k_pr", "k_ld", "k_tc", "k_vc", "k_t1",
                         "k_vm", "k_t2", "k_t3", "k_t4", "k_t5", "k_rl",
                         "k_sv", "k_sc", "k_ma", "k_mx", "k_dv", "k_vx",
                         "k_tp", "k_ts", "k_r1", "k_r2"))
    NC2 = globals().get('K_NC2', NT1 // 2)
    NHL = globals().get('K_NH', 16)
    PREP = globals().get('K_PREP', 99)
    NCH = NC2 * 2
    import concourse.bass as bass_m

    def eng_gpsimd(e):
        for sm in (s_vm, s_dv, s_rl, s_ma, s_sv, s_sc, s_t2, s_tc, s_vc):
            e.sem_inc(sm, 64)
        e.sem_inc(s_tp, 128)
        e.sem_inc(s_ts, 16)
        e.dma_start(out=mnsb[:], in_=aps["mn_r"]).then_inc(s_w, 16)
        e.dma_start(out=memsb[:], in_=aps["mem_r"]).then_inc(s_w, 16)
        e.dma_start(out=sqssb[:], in_=aps["sqs"]).then_inc(s_w, 16)
        e.dma_start(out=gater[:], in_=aps["gate"][None, :]).then_inc(s_w, 16)
        e.dma_start(out=minc[:], in_=aps["m_incu"]).then_inc(s_w, 16)
        e.dma_start(out=ones1[:], in_=aps["ones1b"][0:1, :]).then_inc(s_w, 16)
        e.dma_start(out=onesc[:], in_=aps["ones_col"]).then_inc(s_w, 16)
        e.memset(Ssb[:], 0.0)
        e.memset(Sbf[:], 0.0)
        e.memset(vext[:], 1.0)
        e.nop().then_inc(s_w, 3)
        for n in range(NCH):
            par = n % 2
            e.wait_ge(s_tp, 16 * 8 * (n + 1) + 128)
            e.dma_start(out=outD[:, :, n * 128:(n + 1) * 128].rearrange(
                "d p t -> p d t"), in_=kttm[:, par]).then_inc(s_ts, 16)
        e.wait_ge(s_ts, 16 + 16 * NCH)

    def eng_sync(e):
        for n in range(NCH):
            par = n % 2
            e.wait_ge(s_tc, 64 + max(0, n - 1))
            e.wait_ge(s_vc, 64 + max(0, n - 1))
            e.dma_start(out=sqc[:, par],
                        in_=sqD[:, :, n * 128:(n + 1) * 128].rearrange(
                            "d p t -> p d t")).then_inc(s_ld, 16)
            e.dma_start(out=skc[:, par],
                        in_=skD[:, :, n * 128:(n + 1) * 128].rearrange(
                            "d p t -> p d t")).then_inc(s_ld, 16)
            e.dma_start(out=sktc[:, par], in_=sktD[n]).then_inc(s_ld, 16)
            e.dma_start(out=vtc[:, par], in_=vtD[n]).then_inc(s_ld, 16)

    def eng_scalar(e):
        mks = lambda x: ctx.mark("scalar", x)
        wts = lambda n_: ctx.wt("scalar", e, n_)
        if PREP < 2:
            return
        e.wait_ge(s_w, 16 * 7 + 3)
        c1 = mks(e.activation(sqsbf[:], sqssb[:], AF.Copy))
        wts(c1)
        c2 = mks(e.activation(gsig[:], gater[:], AF.Sigmoid))
        wts(c2)
        c3 = mks(e.activation(gsbf[:], gsig[:], AF.Copy))
        wts(c3)
        e.nop().then_inc(s_pr, 1)
        if PREP < 4:
            return
        e.wait_ge(s_pr, 2)
        c4 = mks(e.activation(wexp[:], wrow[:], AF.Exp))
        wts(c4)
        e.nop().then_inc(s_pr, 1)
        if NC2 == 0:
            return
        with e.Fori(0, NC2, 1) as np2:
            for par in range(2):
                base16 = (np2 * 2 + par) * NHL
                for h in range(NHL):
                    pb, blk = (h % 2) * 64, h // 2
                    ctx.rwait(e, s_sv, base16, h + 1 + 64)
                    cc = mks(e.activation(Sbf[pb:pb + 64, blk, :],
                                            Ssb[pb:pb + 64, blk, :], AF.Copy))
                    wts(cc)
                    e.nop().then_inc(s_sc, 1)
                ctx.rwait(e, s_mx, base16, NHL)
                ctx.rwait(e, s_ts, np2 * 32, par * 16 + 16)
                for fb in range(8):
                    d = e.dma_start_transpose(
                        kttm[:, par, fb], mixs[:, par, fb * 128:(fb + 1) * 128])
                    d.then_inc(s_tp, 16)

    def eng_tensor(e):
        if PREP < 3:
            return
        e.wait_ge(s_pr, 1)
        for h in range(16):
            pb, blk = (h % 2) * 64, h // 2
            pbank = pREL if h % 2 == 0 else pSUP
            if h >= 2:
                e.wait_ge(s_r2, h - 1)
            e.matmul(pbank[0:1, 0:4], sqsbf[pb:pb + 64, blk:blk + 1],
                     mnsb[pb:pb + 64, blk, :], start=True, stop=True
                     ).then_inc(s_r1, 1)
        if PREP < 5:
            return
        e.wait_ge(s_pr, 4)
        e.matmul(pMEM[:, 0, 0:64], ones1[:], wbf[:], start=True, stop=True
                 ).then_inc(s_t4, 1)
        e.matmul(pMEM[:, 1, 0:16], ones1[:], gsbf[:], start=True, stop=True
                 ).then_inc(s_t4, 1)
        e.wait_ge(s_pr, 5)
        if NC2 == 0:
            return
        with e.Fori(0, NC2, 1) as np2:
            for par in range(2):
                base16 = (np2 * 2 + par) * NHL
                base64m = (np2 * 2 + par) * 4 * NHL
                baseld = np2 * 128 + par * 64
                for h in range(NHL):
                    hp = h % 2
                    pb, blk = (h % 2) * 64, h // 2
                    ctx.rwait(e, s_ld, baseld, 64)
                    ctx.rwait(e, s_vm, base16, h - 1 + 64)
                    e.matmul(pAT[:, hp, 0:128], skc[pb:pb + 64, par, blk, :],
                             sqc[pb:pb + 64, par, blk, :], start=True,
                             stop=True).then_inc(s_t1, 1)
                    ctx.rwait(e, s_vm, base16, h + 1 + 64)
                    ctx.rwait(e, s_dv, base16, h - 1 + 64)
                    ctx.rwait(e, s_vx, np2 * 2, par + 1)
                    ctx.rwait(e, s_sc, base16, h + 1 + 64 - NHL)
                    e.matmul(pLOC[:, hp, 0:65], atm[:, hp, :],
                             vext[:, par, h, :], start=True, stop=False)
                    e.matmul(pLOC[:, hp, 0:65], sqc[pb:pb + 64, par, blk, :],
                             Sbf[pb:pb + 64, blk, :], start=False, stop=True
                             ).then_inc(s_t2, 1)
                    ctx.rwait(e, s_rl, base16, h + 64)
                    e.matmul(pREL[:, 0:4], sqc[pb:pb + 64, par, blk, :],
                             mnsb[pb:pb + 64, blk, :], start=True, stop=True
                             ).then_inc(s_t3, 1)
                    for m in range(4):
                        ctx.rwait(e, s_ma, base64m, 4 * h + m - 1 + 64)
                        e.matmul(pMEM[:, m % 2, 0:64],
                                 sqc[pb:pb + 64, par, blk, :],
                                 memsb[pb:pb + 64, blk, m, :], start=True,
                                 stop=True).then_inc(s_t4, 1)
                    ctx.rwait(e, s_sv, base16, h + 64)
                    e.matmul(pSUP[0:64, 0:65],
                             sktc[:, par, h * 64:(h + 1) * 64],
                             vext[:, par, h, :], start=True, stop=True
                             ).then_inc(s_t5, 1)
                e.nop().then_inc(s_tc, 1)

    def eng_vector(e):
        mkv = lambda x: ctx.mark("vector", x)
        wtv = lambda n_: ctx.wt("vector", e, n_)
        if PREP < 4:
            return
        for h in range(16):
            e.wait_ge(s_r1, h + 1)
            cr = mkv(e.tensor_copy(wrow0[0:1, h * 4:(h + 1) * 4],
                                   (pREL if h % 2 == 0 else pSUP)[0:1, 0:4]))
            wtv(cr)
            e.engine_nop().then_inc(s_r2, 1)
        c1 = mkv(e.tensor_scalar(wrow[:], wrow0[:], 1.0 / T, None,
                                 ALU.mult))
        wtv(c1)
        w3 = wrow[:].rearrange("o (h m) -> o h m", m=4)
        c2 = mkv(e.reduce_max(wtmp[:, :, 0:1], w3, axis=AX.X))
        wtv(c2)
        a1, a2 = bass_m.broadcast_tensor_aps(w3, wtmp[:, :, 0:1])
        c3 = mkv(e.tensor_tensor(w3, a1, a2, ALU.subtract))
        wtv(c3)
        e.engine_nop().then_inc(s_pr, 1)
        e.wait_ge(s_pr, 3)
        we3 = wexp[:].rearrange("o (h m) -> o h m", m=4)
        c4 = mkv(e.reduce_sum(wtmp[:, :, 1:2], we3, axis=AX.X))
        wtv(c4)
        c5 = mkv(e.reciprocal(wtmp[:, :, 1:2], wtmp[:, :, 1:2]))
        wtv(c5)
        b1, b2 = bass_m.broadcast_tensor_aps(we3, wtmp[:, :, 1:2])
        wf3 = wfin[:].rearrange("o (h m) -> o h m", m=4)
        c6 = mkv(e.tensor_tensor(wf3, b1, b2, ALU.mult))
        wtv(c6)
        c7 = mkv(e.tensor_copy(wbf[:], wfin[:]))
        wtv(c7)
        e.engine_nop().then_inc(s_pr, 1)
        if PREP < 5:
            return
        e.wait_ge(s_t4, 2)
        c8 = mkv(e.tensor_copy(wbc[:], pMEM[:, 0, 0:64]))
        c9 = mkv(e.tensor_copy(gbc[:], pMEM[:, 1, 0:16]))
        wtv(c9)
        e.engine_nop().then_inc(s_pr, 1)
        if NC2 == 0:
            return
        with e.Fori(0, NC2, 1) as np2:
            for par in range(2):
                n_rt = np2 * 2 + par
                base16 = n_rt * NHL
                base64m = n_rt * 4 * NHL
                baseld = np2 * 128 + par * 64
                ctx.rwait(e, s_ld, baseld, 64)
                ctx.rwait(e, s_tc, np2 * 2, par - 1 + 64)
                cv = mkv(e.tensor_copy(
                    vext[:, par, :, 0:64],
                    vtc[:, par].rearrange("p (h q) -> p h q", q=64)))
                wtv(cv)
                e.engine_nop().then_inc(s_vx, 1)
                ctx.rwait(e, s_tp, np2 * 256, par * 128 - 128 + 128)
                for h in range(NHL):
                    hp = h % 2
                    pb, blk = (h % 2) * 64, h // 2
                    ctx.rwait(e, s_t1, base16, h + 1)
                    ctx.rwait(e, s_t2, base16, h - 1 + 64)
                    ca = mkv(e.tensor_tensor(atm[:, hp, :], pAT[:, hp, 0:128],
                                             minc[:], ALU.mult))
                    wtv(ca)
                    e.engine_nop().then_inc(s_vm, 1)
                    ctx.rwait(e, s_t2, base16, h + 1 + 64)
                    cb = mkv(e.tensor_scalar(dcl[:], pLOC[:, hp, 64:65],
                                             1e-6, None, ALU.max))
                    wtv(cb)
                    cc = mkv(e.reciprocal(dinv[:], dcl[:]))
                    wtv(cc)
                    cd = mkv(e.tensor_scalar(ltm[:], pLOC[:, hp, 0:64],
                                             dinv[:, 0:1], None, ALU.mult))
                    wtv(cd)
                    e.engine_nop().then_inc(s_dv, 1)
                    ctx.rwait(e, s_t3, base16, h + 1)
                    ce = mkv(e.tensor_scalar(relc[:], pREL[:, 0:4], 1e-6,
                                             None, ALU.max))
                    wtv(ce)
                    cf = mkv(e.reciprocal(wrel[:], relc[:]))
                    wtv(cf)
                    cg = mkv(e.tensor_tensor(wrel[:], wrel[:],
                                             wbc[:, 4 * h:4 * h + 4],
                                             ALU.mult))
                    wtv(cg)
                    e.engine_nop().then_inc(s_rl, 1)
                    for m in range(4):
                        ctx.rwait(e, s_t4, base64m, 4 * h + m + 3)
                        if m == 0:
                            ch = mkv(e.tensor_scalar(macc[:],
                                                     pMEM[:, 0, 0:64],
                                                     wrel[:, 0:1], None,
                                                     ALU.mult))
                        else:
                            ch = mkv(e.scalar_tensor_tensor(
                                macc[:], pMEM[:, m % 2, 0:64],
                                wrel[:, m:m + 1], macc[:], ALU.mult,
                                ALU.add))
                        wtv(ch)
                        e.engine_nop().then_inc(s_ma, 1)
                    ci = mkv(e.tensor_tensor(dif[:], macc[:], ltm[:],
                                             ALU.subtract))
                    wtv(ci)
                    cj = mkv(e.scalar_tensor_tensor(
                        mixs[:, par, h * 64:(h + 1) * 64], dif[:],
                        gbc[:, h:h + 1], ltm[:], ALU.mult, ALU.add))
                    wtv(cj)
                    e.engine_nop().then_inc(s_mx, 1)
                    ctx.rwait(e, s_t5, base16, h + 1)
                    ck = mkv(e.tensor_tensor(Ssb[pb:pb + 64, blk, :],
                                             Ssb[pb:pb + 64, blk, :],
                                             pSUP[0:64, 0:65], ALU.add))
                    wtv(ck)
                    e.engine_nop().then_inc(s_sv, 1)
                e.engine_nop().then_inc(s_vc, 1)

    return dict(gpsimd=eng_gpsimd, sync=eng_sync, scalar=eng_scalar,
                tensor=eng_tensor, vector=eng_vector)


def specs_phase_k():
    inputs = {
        "sq_fm": ([8, 128, T], BF16), "sk_fm": ([8, 128, T], BF16),
        "skt_tm": ([NT1, 128, D], BF16), "v_tm": ([NT1, 128, D], BF16),
        "sqs": ([128, 8], F32), "gate": ([16], F32),
        "mn_r": ([128, 8, 4], F32), "mem_r": ([128, 8, 4, 64], F32),
        "m_incu": ([128, 128], BF16),
        "ones1b": ([128, 128], BF16), "ones_col": ([128, 1], BF16),
    }
    outputs = {"mixed_fm": ([8, 128, T], BF16)}
    sbuf = {
        "k_sqc": ([128, 2, 8, 128], BF16), "k_skc": ([128, 2, 8, 128], BF16),
        "k_sktc": ([128, 2, 1024], BF16), "k_vtc": ([128, 2, 1024], BF16),
        "k_vext": ([128, 2, 16, 65], BF16),
        "k_atm": ([128, 2, 128], BF16),
        "k_Ssb": ([128, 8, 65], F32), "k_Sbf": ([128, 8, 65], BF16),
        "k_mnsb": ([128, 8, 4], BF16), "k_memsb": ([128, 8, 4, 64], BF16),
        "k_wbc": ([128, 64], F32), "k_gbc": ([128, 16], F32),
        "k_sqssb": ([128, 8], F32), "k_sqsbf": ([128, 8], BF16),
        "k_gater": ([1, 16], F32), "k_gsig": ([1, 16], F32),
        "k_gsbf": ([1, 16], BF16),
        "k_wrow": ([1, 64], F32), "k_wrow0": ([1, 64], F32),
        "k_wexp": ([1, 64], F32),
        "k_wfin": ([1, 64], F32), "k_wtmp": ([1, 16, 2], F32),
        "k_wbf": ([1, 64], BF16),
        "k_relc": ([128, 4], F32), "k_wrel": ([128, 4], F32),
        "k_dinv": ([128, 1], F32), "k_dcl": ([128, 1], F32),
        "k_ltm": ([128, 64], F32), "k_macc": ([128, 64], F32),
        "k_dif": ([128, 64], F32),
        "k_mixs": ([128, 2, 1024], BF16), "k_kttm": ([128, 2, 8, 128], BF16),
        "k_minc": ([128, 128], BF16),
        "k_ones1": ([1, 128], BF16), "k_onesc": ([128, 1], BF16),
    }
    psum = {
        "k_pAT": ([128, 2, 512], F32), "k_pLOC": ([128, 2, 512], F32),
        "k_pMEM": ([128, 2, 512], F32), "k_pSUP": ([128, 512], F32),
        "k_pREL": ([128, 512], F32),
    }
    sems = ["k_w", "k_pr", "k_ld", "k_tc", "k_vc", "k_t1", "k_vm", "k_t2",
            "k_t3", "k_t4", "k_t5", "k_rl", "k_sv", "k_sc", "k_ma", "k_mx",
            "k_dv", "k_vx", "k_tp", "k_ts", "k_r1", "k_r2"]
    return inputs, outputs, sbuf, psum, sems


def phase_l(ctx, sb, ps, sem):
    aps = ctx.aps
    mxD = aps["mixed_fm"]
    wosb = sb["l_wo"]; mxc = sb["l_mxc"]; y0c = sb["l_y0c"]; yo = sb["l_yo"]
    pO = ps["l_pO"]
    s_w, s_x, s_mm, s_ev, s_st0, s_st1 = (
        sem[k] for k in ("l_w", "l_x", "l_mm", "l_ev", "l_st0", "l_st1"))

    def eng_gpsimd(e):
        e.sem_inc(s_st0, 16)
        e.sem_inc(s_st1, 16)
        e.dma_start(out=wosb[:],
                    in_=aps["wo"].rearrange("(db p) f -> p db f", p=128)
                    ).then_inc(s_w, 16)
        for n in range(NT1):
            par = n % 2
            e.wait_ge(s_ev, 2 * (n + 1))
            e.dma_start(out=aps["y1a"][n * 128:(n + 1) * 128, :],
                        in_=yo[:, par]).then_inc(s_st1 if par else s_st0, 16)
        e.wait_ge(s_st0, 16 + 16 * (NT1 // 2))
        e.wait_ge(s_st1, 16 + 16 * (NT1 // 2))

    def eng_sync(e):
        for n in range(NT1):
            par = n % 2
            if n >= 2:
                e.wait_ge(s_mm, 2 * (n - 1))
                e.wait_ge(s_ev, 2 * (n - 1))
            e.dma_start(out=mxc[:, par],
                        in_=mxD[:, :, n * 128:(n + 1) * 128].rearrange(
                            "d p t -> p d t")).then_inc(s_x, 16)
            e.dma_start(out=y0c[:, par], in_=aps["yres"][n * 128:(n + 1) * 128,
                                                         :]).then_inc(s_x, 16)

    def eng_tensor(e):
        e.wait_ge(s_w, 16)
        for n in range(NT1):
            par = n % 2
            e.wait_ge(s_x, 32 * (n + 1))
            if n >= 2:
                e.wait_ge(s_ev, 2 * (n - 1))
            for u in range(2):
                for db in range(8):
                    mm = e.matmul(pO[:, par * 2 + u, :], mxc[:, par, db, :],
                                  wosb[:, db, u * 512:(u + 1) * 512],
                                  start=(db == 0), stop=(db == 7))
                mm.then_inc(s_mm, 1)

    def eng_vector(e):
        for n in range(NT1):
            par = n % 2
            e.wait_ge(s_st1 if par else s_st0, 16 * (n // 2 + 1))
            for u in range(2):
                e.wait_ge(s_mm, 2 * n + u + 1)
                e.tensor_tensor(yo[:, par, u * 512:(u + 1) * 512],
                                pO[:, par * 2 + u, :],
                                y0c[:, par, u * 512:(u + 1) * 512], ALU.add
                                ).then_inc(s_ev, 1)

    return dict(gpsimd=eng_gpsimd, sync=eng_sync, tensor=eng_tensor,
                vector=eng_vector)


def specs_phase_l():
    inputs = {
        "mixed_fm": ([8, 128, T], BF16),
        "wo": ([D, D], F32),
        "yres": ([T, D], F32),
    }
    outputs = {"y1a": ([T, D], F32)}
    sbuf = {
        "l_wo": ([128, 8, D], BF16),
        "l_mxc": ([128, 2, 8, 128], BF16),
        "l_y0c": ([128, 2, D], F32),
        "l_yo": ([128, 2, D], F32),
    }
    psum = {"l_pO": ([128, 4, 512], F32)}
    sems = ["l_w", "l_x", "l_mm", "l_ev", "l_st0", "l_st1"]
    return inputs, outputs, sbuf, psum, sems


def host_prep_l1(z):
    mem = np.asarray(z["L1_attn_memories"], np.float32)
    mn = np.asarray(z["L1_attn_memory_norms"], np.float32)
    mem_r = np.zeros((128, 8, 4, 64), np.float32)
    mn_r = np.zeros((128, 8, 4), np.float32)
    for h in range(16):
        pb, blk = (h % 2) * 64, h // 2
        mem_r[pb:pb + 64, blk] = mem[:, h].transpose(1, 0, 2)
        mn_r[pb:pb + 64, blk] = mn[:, h].transpose(1, 0)
    return {"mem_r": mem_r, "mn_r": mn_r}


def kernel(**z):
    import sys
    import time as _time
    bf = ml_dtypes.bfloat16
    x_full = np.asarray(z["L0_attn_hidden_states"], np.float32)
    eyeb = host_consts()["eyeb"]
    ones1b = np.ones((128, 128), bf)
    onescol = np.ones((128, 1), bf)
    masks = host_masks()

    def launch(phases, specs, per_core_maps, scratch=None, tag=""):
        inS, outS, sbS, psS, semS = specs
        t0 = _time.time()
        nc = build_program(phases, inS, outS, sbS, psS, semS,
                           scratch_spec=scratch)
        res = run_bass_kernel_spmd(nc, per_core_maps, core_ids=list(range(8)))
        out = [{k: np.asarray(v) for k, v in res.results[c].items()}
               for c in (0, 4)]
        del res
        print("[kernel] %s done %.0fs" % (tag, _time.time() - t0),
              file=sys.stderr, flush=True)
        return out

    def bmaps(mk_map):
        per_b = [mk_map(0), mk_map(1)]
        return [per_b[c // 4] for c in range(8)]

    st = {}

    def keep(res2, names):
        for nm_ in names:
            st[nm_] = [res2[0][nm_], res2[1][nm_]]

    def drop(*names):
        for nm_ in names:
            st.pop(nm_, None)

    inA, outA, sbA, psA, semA = specs_phase_a()
    resA = launch([phase_consts, phase_a], (inA, outA, sbA, psA, semA),
                  bmaps(lambda b: {"x": x_full[b], "eyeb": eyeb}), tag="A0")
    keep(resA, ["xn_fm"])
    del resA

    hb = host_prep_b(z)
    resB = launch([phase_b], specs_phase_b(),
                  bmaps(lambda b: {"xn_fm": st["xn_fm"][b], **hb}), tag="B")
    keep(resB, ["q_fm", "k_fm", "kb_fm", "v_fm", "k_tm", "kb_tm", "vb_tm"])
    del resB

    inC, outC, sbC, psC, semC = specs_phase_c()
    resC = launch([phase_consts, phase_c],
                  (inC, outC, sbC, psC, list(semC) + ["c_done"]),
                  bmaps(lambda b: {"q_fm": st["q_fm"][b],
                                   "k_fm": st["k_fm"][b],
                                   "kb_fm": st["kb_fm"][b],
                                   "k_tm": st["k_tm"][b],
                                   "kb_tm": st["kb_tm"][b],
                                   "vb_tm": st["vb_tm"][b],
                                   "eyeb": eyeb, **masks}), tag="C")
    keep(resC, ["delta_fm"])
    del resC
    drop("q_fm", "k_fm", "kb_fm", "k_tm", "kb_tm", "vb_tm")

    hd = host_prep_d(z)
    resD = launch([phase_d], specs_phase_d(),
                  bmaps(lambda b: {"v_fm": st["v_fm"][b], **hd}), tag="D")
    keep(resD, ["fir%d_fm" % k for k in FIRK])
    del resD

    he = host_prep_eg(z)
    inG, outG, sbG, psG, semG = specs_phase_eg()
    resG = launch([phase_consts, phase_eg],
                  (inG, outG, sbG, psG, list(semG) + ["c_done"]),
                  bmaps(lambda b: {
                      "xn_fm": st["xn_fm"][b], "x": x_full[b],
                      "delta_fm": st["delta_fm"][b], "v_fm": st["v_fm"][b],
                      **{"fir%d_fm" % k: st["fir%d_fm" % k][b] for k in FIRK},
                      **he, "eyeb": eyeb, "ones1b": ones1b,
                      "ones_col": onescol}), tag="EG")
    keep(resG, ["y0"])
    del resG
    drop("delta_fm", "v_fm", *["fir%d_fm" % k for k in FIRK])

    wpk0, w2pk0 = host_pack_ffn(z["L0_ffn_W0"], z["L0_ffn_W1"],
                                z["L0_ffn_b0"], z["L0_ffn_b1"],
                                z["L0_ffn_W2"], z["L0_ffn_b2"])
    nmh = dict(L0FFN_NM)
    nmh["y_init"] = "y0_in"
    inH, outH, sbH, psH, semH = specs_phase_h(nmh)
    inH = dict(inH)
    inH["y0_in"] = ([T, D], F32)
    outH = dict(outH)
    scrH = {nmh["hg"]: outH.pop(nmh["hg"]), nmh["cmb"]: outH.pop(nmh["cmb"])}
    onesr512 = np.ones((1, 512), bf)
    ones1f = np.ones((1, 128), np.float32)
    resH = launch([lambda c, sb, ps, sem: phase_h(c, sb, ps, sem, nmh)],
                  (inH, outH, sbH, psH, semH),
                  bmaps(lambda b: {"xn_fm": st["xn_fm"][b],
                                   "y0_in": st["y0"][b],
                                   "ffn_wpk": wpk0, "ffn_w2pk": w2pk0,
                                   "ffn_Wg": np.asarray(z["L0_ffn_Wg"],
                                                        np.float32),
                                   "ffn_bg": np.asarray(z["L0_ffn_bg"],
                                                        np.float32),
                                   "onesr512": onesr512, "ones1f": ones1f}),
                  scratch=scrH, tag="H0")
    st["y0p"] = [resH[0]["y0p"][YOFF:], resH[1]["y0p"][YOFF:]]
    del resH, wpk0, w2pk0
    drop("xn_fm", "y0")

    resA1 = launch([phase_consts, phase_a], (inA, outA, sbA, psA, semA),
                   bmaps(lambda b: {"x": st["y0p"][b], "eyeb": eyeb}),
                   tag="A1")
    st["xn1_fm"] = [resA1[0]["xn_fm"], resA1[1]["xn_fm"]]
    del resA1

    resJ = launch([phase_j], specs_phase_j(),
                  bmaps(lambda b: {
                      "xn_fm": st["xn1_fm"][b],
                      "ln_w": np.asarray(z["L1_ffn_ln_w"], np.float32),
                      "ones1b": ones1b, "ones_col": onescol,
                      "c_epsf": np.full((1, 1), np.finfo(np.float32).eps,
                                        np.float32)}), tag="J")
    st["xnr_fm"] = [resJ[0]["xnr_fm"], resJ[1]["xnr_fm"]]
    del resJ

    wqT = np.ascontiguousarray(np.asarray(z["L1_attn_w_q"], np.float32).T)
    wkT = np.ascontiguousarray(np.asarray(z["L1_attn_w_k"], np.float32).T)
    wvT = np.ascontiguousarray(np.asarray(z["L1_attn_w_v"], np.float32).T)
    resI = launch([phase_i], specs_phase_i(),
                  bmaps(lambda b: {"xn_fm": st["xn1_fm"][b], "wq": wqT,
                                   "wk": wkT, "wv": wvT}), tag="I")
    keep(resI, ["sq_fm", "sk_fm", "skt_tm", "v_tm", "sqs"])
    del resI
    drop("xn1_fm")

    hl1 = host_prep_l1(z)
    resK = launch([phase_k], specs_phase_k(),
                  bmaps(lambda b: {
                      "sq_fm": st["sq_fm"][b], "sk_fm": st["sk_fm"][b],
                      "skt_tm": st["skt_tm"][b], "v_tm": st["v_tm"][b],
                      "sqs": st["sqs"][b],
                      "gate": np.asarray(z["L1_attn_gate"], np.float32),
                      **hl1, "m_incu": masks["m_incu"],
                      "ones1b": ones1b, "ones_col": onescol}), tag="K")
    st["mixed_fm"] = [resK[0]["mixed_fm"], resK[1]["mixed_fm"]]
    del resK
    drop("sq_fm", "sk_fm", "skt_tm", "v_tm", "sqs")

    woT = np.ascontiguousarray(np.asarray(z["L1_attn_w_o"], np.float32).T)
    resL = launch([phase_l], specs_phase_l(),
                  bmaps(lambda b: {"mixed_fm": st["mixed_fm"][b], "wo": woT,
                                   "yres": st["y0p"][b]}), tag="L")
    st["y1a"] = [resL[0]["y1a"], resL[1]["y1a"]]
    del resL
    drop("mixed_fm")

    wpk1, w2pk1 = host_pack_ffn(z["L1_ffn_W1b"], z["L1_ffn_W1a"],
                                z["L1_ffn_b1b"], z["L1_ffn_b1a"],
                                z["L1_ffn_W2"], z["L1_ffn_b2"])
    resH1 = launch([lambda c, sb, ps, sem: phase_h(c, sb, ps, sem, nmh)],
                   (inH, outH, sbH, psH, semH),
                   bmaps(lambda b: {"xn_fm": st["xnr_fm"][b],
                                    "y0_in": st["y1a"][b],
                                    "ffn_wpk": wpk1, "ffn_w2pk": w2pk1,
                                    "ffn_Wg": np.asarray(z["L1_ffn_Wg"],
                                                         np.float32),
                                    "ffn_bg": np.asarray(z["L1_ffn_bg"],
                                                         np.float32),
                                    "onesr512": onesr512,
                                    "ones1f": ones1f}),
                   scratch=scrH, tag="H1")
    out = np.stack([resH1[0]["y0p"][YOFF:], resH1[1]["y0p"][YOFF:]])
    return out
